# version 02
# baseline (speedup 1.0000x reference)
import numpy as np
from contextlib import ExitStack

import concourse.bass as bass
import concourse.mybir as mybir
from concourse.bass_utils import run_bass_kernel_spmd

ALU = mybir.AluOpType
AF = mybir.ActivationFunctionType
F32 = mybir.dt.float32
BF16 = mybir.dt.bfloat16

ENGINES = ("tensor", "vector", "scalar", "gpsimd", "sync")


class Prog:
    def __init__(self, nc):
        self.nc = nc
        self.ops = []
        self.last_write = {}
        self.readers = {}
        self.stream_count = {}
        self.stream_last = {}
        self.n_cc = 0

    def op(self, engine, fn, reads=(), writes=(), dma=False, cc=False,
           stream=None, tag=""):
        i = len(self.ops)
        deps = set()
        for r in reads:
            w = self.last_write.get(r)
            if w is not None:
                deps.add(w)
        for w_ in writes:
            w = self.last_write.get(w_)
            if w is not None:
                deps.add(w)
            deps.update(self.readers.get(w_, ()))
        o = dict(i=i, engine=engine, fn=fn, dma=dma, cc=cc, tag=tag,
                 needed=False, skey=None, dval=None)
        if dma:
            skey = stream if stream is not None else (writes[0] if writes else f"_dma{i}")
            skey = (engine, skey)
            prev = self.stream_last.get(skey)
            if prev is not None:
                deps.add(prev)
        if cc:
            o["cc_idx"] = self.n_cc
            self.n_cc += 1
        deps.discard(i)
        dma_waits = {}
        eng_deps = set()
        for d in deps:
            p = self.ops[d]
            if p["dma"]:
                k = p["skey"]
                v = 16 * self.stream_count[k]
                if dma_waits.get(k, 0) < v:
                    dma_waits[k] = v
            else:
                eng_deps.add(d)
        o["dma_waits"] = dma_waits
        o["eng_deps"] = eng_deps
        if dma:
            self.stream_count[skey] = self.stream_count.get(skey, 0) + 1
            self.stream_last[skey] = i
            o["skey"] = skey
            o["dval"] = 16 * self.stream_count[skey]
        self.ops.append(o)
        for r in reads:
            self.readers.setdefault(r, []).append(i)
        for w_ in writes:
            self.last_write[w_] = i
            self.readers[w_] = []
        return i

    def dma(self, queue, out, in_, reads=(), writes=(), stream=None, tag="", **kw):
        return self.op(queue, lambda e: e.dma_start(out=out, in_=in_, **kw),
                       reads=reads, writes=writes, dma=True, stream=stream, tag=tag)

    def mm(self, emit, reads=(), writes=(), tag=""):
        return self.op("tensor", emit, reads=reads, writes=writes, tag=tag)

    def finalize(self, final_regions=(), final_engine="sync"):
        nc = self.nc
        for o in self.ops:
            for d in o["eng_deps"]:
                self.ops[d]["needed"] = True
        final_deps = set()
        for r in final_regions:
            w = self.last_write.get(r)
            if w is not None:
                final_deps.add(w)
        for d in final_deps:
            self.ops[d]["needed"] = True

        eng_sem = {e: nc.alloc_semaphore(f"esem_{e}") for e in ENGINES}
        stream_sem = {k: nc.alloc_semaphore(f"dsem_{j}")
                      for j, k in enumerate(sorted(self.stream_count, key=str))}
        cc_sems = [nc.alloc_semaphore(f"ccsem_{k}") for k in range(self.n_cc)]

        eng_rank = {e: 0 for e in ENGINES}
        for o in self.ops:
            e = o["engine"]
            if o["cc"]:
                o["sem"], o["val"] = cc_sems[o["cc_idx"]], 1
            elif o["dma"]:
                o["sem"], o["val"] = stream_sem[o["skey"]], o["dval"]
            elif o["needed"]:
                eng_rank[e] += 1
                o["sem"], o["val"] = eng_sem[e], eng_rank[e]
            else:
                o["sem"], o["val"] = None, None

        waited = {e: {} for e in ENGINES}
        n_waits = 0
        for o in self.ops:
            e = o["engine"]
            need = {}
            for k, v in o["dma_waits"].items():
                need[stream_sem[k]] = max(need.get(stream_sem[k], 0), v)
            for d in o["eng_deps"]:
                p = self.ops[d]
                if p["cc"]:
                    s, v = p["sem"], p["val"]
                else:
                    s, v = p["sem"], p["val"]
                assert s is not None
                need[s] = max(need.get(s, 0), v)
            waits = []
            for sem, v in sorted(need.items(), key=lambda kv: str(kv[0])):
                if waited[e].get(sem, 0) < v:
                    waits.append((sem, v))
                    waited[e][sem] = v
            o["waits"] = waits
            n_waits += len(waits)

        fin = {}
        for d in sorted(final_deps):
            p = self.ops[d]
            if p["dma"]:
                v = 16 * self.stream_count[p["skey"]]
                fin[p["sem"]] = max(fin.get(p["sem"], 0), v)
            else:
                fin[p["sem"]] = max(fin.get(p["sem"], 0), p["val"])
        for k, n in self.stream_count.items():
            s = stream_sem[k]
            fin[s] = max(fin.get(s, 0), 16 * n)

        streams = {e: [o for o in self.ops if o["engine"] == e] for e in ENGINES}

        with nc.Block() as block:
            def mk(e_name):
                def run(eng):
                    for o in streams[e_name]:
                        for sem, v in o["waits"]:
                            eng.wait_ge(sem, v)
                        inst = o["fn"](eng)
                        if isinstance(inst, (list, tuple)):
                            inst = inst[-1]
                        if o["sem"] is not None:
                            if o["dma"]:
                                inst.then_inc(o["sem"], 16)
                            elif o["cc"]:
                                inst.then_inc(o["sem"])
                            else:
                                inst.then_inc(o["sem"], 1)
                    if e_name == final_engine:
                        for sem, v in sorted(fin.items(), key=lambda kv: str(kv[0])):
                            eng.wait_ge(sem, v)
                return run

            for e_name, setter in (("sync", block.sync), ("tensor", block.tensor),
                                   ("vector", block.vector), ("scalar", block.scalar),
                                   ("gpsimd", block.gpsimd)):
                if streams[e_name] or e_name == final_engine:
                    setter(mk(e_name))
        return dict(n_ops=len(self.ops), n_waits=n_waits,
                    n_streams=len(stream_sem), eng_rank=eng_rank)



def cdiv(a, b):
    return (a + b - 1) // b


class Ctx:

    def __init__(self, nc, es):
        self.nc = nc
        self.es = es
        self.p = Prog(nc)
        self.b = {}
        self.uid = 0

    def sb(self, name, shape, dt=F32):
        h = self.es.enter_context(self.nc.sbuf_tensor(name, shape, dt))
        self.b[name] = h
        return h

    def ps(self, name, shape, dt=F32):
        h = self.es.enter_context(self.nc.psum_tensor(name, shape, dt))
        self.b[name] = h
        return h

    def u(self):
        self.uid += 1
        return self.uid


def emit_cast_bf16(ctx, dst, dst_r, src, src_r, n_chunks, T, engine="vector"):
    p = ctx.p
    for c in range(n_chunks):
        p.op(engine, lambda e, c=c: e.tensor_copy(dst[c][:, :T], src[c][:, :T]),
             reads=[src_r + str(c)], writes=[dst_r + str(c)])


def emit_ln_fm(ctx, x, x_r, xln, xln_r, T, lo=None, lo_r=None,
               eps=1e-5, n_chunks=8, D_total=1024, xsl=None):
    p = ctx.p
    s0 = xsl or 0
    ones_col = ctx.b["ones_col_bf"]
    ones_row = ctx.b["ones_row_bf"]
    hi, lo_s, sq = ctx.b["ln_hi"], ctx.b["ln_lo"], ctx.b["ln_sq"]
    rows, bc, tmp = ctx.b["ln_rows"], ctx.b["ln_bc"], ctx.b["ln_tmp"]
    pr, pr2, pbc = ctx.b["pb4"], ctx.b["pb5"], ctx.b["pb6"]
    NT = cdiv(T, 512)

    for nt in range(NT):
        t0 = nt * 512
        w = min(512, T - t0)
        for c in range(n_chunks):
            first, last = c == 0, c == n_chunks - 1
            p.op("vector", lambda e, c=c, t0=t0, w=w: e.tensor_copy(
                hi[:, :w], x[c][:, s0 + t0:s0 + t0 + w]),
                reads=[x_r + str(c)], writes=["ln_hi"])
            p.op("vector", lambda e, c=c, t0=t0, w=w: e.tensor_tensor(
                lo_s[:, :w], x[c][:, s0 + t0:s0 + t0 + w], hi[:, :w], op=ALU.subtract),
                reads=[x_r + str(c), "ln_hi"], writes=["ln_lo"])
            p.op("scalar", lambda e, c=c, t0=t0, w=w: e.activation(
                sq[:, :w], x[c][:, s0 + t0:s0 + t0 + w], AF.Square),
                reads=[x_r + str(c)], writes=["ln_sq"])
            def mm_sum(e, w=w, first=first, last=last):
                e.matmul(pr[0:1, :w], ones_col[:, 0:1], hi[:, :w], start=first, stop=False)
                return e.matmul(pr[0:1, :w], ones_col[:, 0:1], lo_s[:, :w], start=False, stop=last)
            p.mm(mm_sum, reads=["ln_hi", "ln_lo"], writes=["pb4"])
            p.mm(lambda e, w=w, first=first, last=last: e.matmul(
                pr2[0:1, :w], ones_col[:, 0:1], sq[:, :w], start=first, stop=last),
                reads=["ln_sq"], writes=["pb5"])
        p.op("scalar", lambda e, w=w: e.activation(rows[:, 0:w], pr[0:1, :w], AF.Copy, scale=1.0 / D_total),
             reads=["pb4"], writes=["ln_rows"])
        p.op("scalar", lambda e, w=w: e.activation(rows[:, 1536:1536 + w], pr2[0:1, :w], AF.Copy, scale=1.0 / D_total),
             reads=["pb5"], writes=["ln_rows"])
        p.op("vector", lambda e, w=w: e.tensor_tensor(rows[:, 512:512 + w], rows[:, 0:w], rows[:, 0:w], op=ALU.mult),
             reads=["ln_rows"], writes=["ln_rows"])
        p.op("vector", lambda e, w=w: e.tensor_tensor(rows[:, 1536:1536 + w], rows[:, 1536:1536 + w], rows[:, 512:512 + w], op=ALU.subtract),
             reads=["ln_rows"], writes=["ln_rows"])
        p.op("scalar", lambda e, w=w: e.activation(rows[:, 512:512 + w], rows[:, 1536:1536 + w], AF.Sqrt, bias=ctx.b["eps_col"][0:1, 0:1]),
             reads=["ln_rows", "eps_col"], writes=["ln_rows"])
        p.op("vector", lambda e, w=w: e.reciprocal(rows[:, 512:512 + w], rows[:, 512:512 + w]),
             reads=["ln_rows"], writes=["ln_rows"])
        p.op("vector", lambda e, w=w: e.tensor_tensor(rows[:, 1024:1024 + w], rows[:, 0:w], rows[:, 512:512 + w], op=ALU.mult),
             reads=["ln_rows"], writes=["ln_rows"])
        p.op("vector", lambda e, w=w: e.tensor_scalar_mul(rows[:, 1024:1024 + w], rows[:, 1024:1024 + w], -1.0),
             reads=["ln_rows"], writes=["ln_rows"])
        p.mm(lambda e, w=w: e.matmul(pbc[:, :w], ones_row[0:1, :], rows[0:1, 512:512 + w], start=True, stop=True),
             reads=["ln_rows"], writes=["pb6"])
        p.op("vector", lambda e, w=w: e.tensor_copy(bc[:, 0:w], pbc[:, :w]),
             reads=["pb6"], writes=["ln_bc"])
        p.mm(lambda e, w=w: e.matmul(pbc[:, :w], ones_row[0:1, :], rows[0:1, 1024:1024 + w], start=True, stop=True),
             reads=["ln_rows"], writes=["pb6"])
        p.op("vector", lambda e, w=w: e.tensor_copy(bc[:, 512:512 + w], pbc[:, :w]),
             reads=["pb6"], writes=["ln_bc"])
        for c in range(n_chunks):
            p.op("vector", lambda e, c=c, t0=t0, w=w: e.tensor_tensor(
                tmp[:, :w], x[c][:, s0 + t0:s0 + t0 + w], bc[:, 0:w], op=ALU.mult),
                reads=[x_r + str(c), "ln_bc"], writes=["ln_tmp"])
            if lo is None:
                p.op("vector", lambda e, c=c, t0=t0, w=w: e.tensor_tensor(
                    xln[c][:, t0:t0 + w], tmp[:, :w], bc[:, 512:512 + w], op=ALU.add),
                    reads=["ln_tmp", "ln_bc"], writes=[xln_r + str(c)])
            else:
                p.op("vector", lambda e, c=c, t0=t0, w=w: e.tensor_tensor(
                    tmp[:, :w], tmp[:, :w], bc[:, 512:512 + w], op=ALU.add),
                    reads=["ln_tmp", "ln_bc"], writes=["ln_tmp"])
                p.op("vector", lambda e, c=c, t0=t0, w=w: e.tensor_copy(xln[c][:, t0:t0 + w], tmp[:, :w]),
                    reads=["ln_tmp"], writes=[xln_r + str(c)])
                p.op("vector", lambda e, c=c, t0=t0, w=w: e.tensor_tensor(
                    lo[c][:, t0:t0 + w], tmp[:, :w], xln[c][:, t0:t0 + w], op=ALU.subtract),
                    reads=["ln_tmp", xln_r + str(c)], writes=[lo_r + str(c)])


def alloc_ln_scratch(ctx, TMAX):
    ctx.sb("ones_col_bf", [128, 1], BF16)
    ctx.sb("ones_row_bf", [1, 128], F32)
    ctx.sb("ln_hi", [128, 512], BF16)
    ctx.sb("ln_lo", [128, 512], BF16)
    ctx.sb("ln_sq", [128, 512], BF16)
    ctx.sb("ln_tmp", [128, 512], F32)
    ctx.sb("ln_rows", [1, 2048], F32)
    ctx.sb("ln_bc", [128, 1024], F32)
    ctx.sb("eps_col", [128, 1], F32)
    ctx.p.op("vector", lambda e: e.memset(ctx.b["eps_col"][:], 1e-5), writes=["eps_col"])
    ctx.p.op("vector", lambda e: e.memset(ctx.b["ones_col_bf"][:], 1.0), writes=["ones_col_bf"])
    ctx.p.op("vector", lambda e: e.memset(ctx.b["ones_row_bf"][:], 1.0), writes=["ones_row_bf"])


def alloc_psum_pool(ctx):
    for j in range(7):
        ctx.ps(f"pb{j}", [128, 512], F32)
    ctx.ps("pbt", [128, 256], BF16)



def prelay_w(W, n_fi, n_fo):
    IN, OUT = W.shape
    assert IN == n_fi * 128 and OUT == n_fo * 128
    return np.ascontiguousarray(
        W.reshape(n_fi, 128, n_fo, 128).transpose(2, 1, 0, 3).reshape(n_fo, 128, n_fi * 128))


def emit_gate_comb(ctx, xln, xln_r, gate_sb, T, tile0):
    p = ctx.p
    pb = ctx.b["pb6"]
    comb_tm = ctx.b["comb_tm"]
    mrow = ctx.b["gate_m"]
    expv = ctx.b["gate_exp"]
    ident = ctx.b["ident_f32"]
    comb_fm = ctx.b["comb_fm"]
    NTC = T // 128
    for tc in range(NTC):
        t0 = tile0 + tc * 128
        def mm_logits(e, t0=t0):
            r = None
            for fi in range(8):
                r = e.matmul(pb[:, 0:8], xln[fi][:, t0:t0 + 128], gate_sb[:, fi * 8:(fi + 1) * 8],
                             start=fi == 0, stop=fi == 7)
            return r
        p.mm(mm_logits, reads=[xln_r + str(c) for c in range(8)] + ["gate_sb"], writes=["pb6"])
        p.op("vector", lambda e: e.tensor_copy(comb_tm[:, 0:8], pb[:, 0:8]),
             reads=["pb6"], writes=["comb_tm"])
        p.op("vector", lambda e: e.reduce_max(mrow[:, 0:1], comb_tm[:, 0:8], axis=mybir.AxisListType.X),
             reads=["comb_tm"], writes=["gate_m"])
        p.op("vector", lambda e: e.tensor_scalar(expv[:, 0:8], comb_tm[:, 0:8], mrow[:, 0:1], None, op0=ALU.is_ge),
             reads=["comb_tm", "gate_m"], writes=["gate_exp"])
        p.op("vector", lambda e: e.scalar_tensor_tensor(expv[:, 0:8], expv[:, 0:8], -1e30, comb_tm[:, 0:8],
                                                        op0=ALU.mult, op1=ALU.add),
             reads=["gate_exp", "comb_tm"], writes=["gate_exp"])
        p.op("vector", lambda e: e.reduce_max(mrow[:, 1:2], expv[:, 0:8], axis=mybir.AxisListType.X),
             reads=["gate_exp"], writes=["gate_m"])
        p.op("vector", lambda e: e.tensor_scalar_mul(mrow[:, 2:3], mrow[:, 0:1], -1.0),
             reads=["gate_m"], writes=["gate_m"])
        p.op("scalar", lambda e: e.activation(expv[:, 0:8], comb_tm[:, 0:8], AF.Exp,
                                              bias=mrow[:, 2:3], accum_out=mrow[:, 3:4]),
             reads=["comb_tm", "gate_m"], writes=["gate_exp", "gate_m"])
        p.op("vector", lambda e: e.reciprocal(mrow[:, 2:3], mrow[:, 3:4]),
             reads=["gate_m"], writes=["gate_m"])
        p.op("vector", lambda e: e.tensor_scalar(comb_tm[:, 0:8], comb_tm[:, 0:8], mrow[:, 1:2], None, op0=ALU.is_ge),
             reads=["comb_tm", "gate_m"], writes=["comb_tm"])
        p.op("vector", lambda e: e.tensor_scalar(expv[:, 0:8], expv[:, 0:8], mrow[:, 2:3], None, op0=ALU.mult),
             reads=["gate_exp", "gate_m"], writes=["gate_exp"])
        p.op("vector", lambda e: e.tensor_tensor(comb_tm[:, 0:8], comb_tm[:, 0:8], expv[:, 0:8], op=ALU.mult),
             reads=["comb_tm", "gate_exp"], writes=["comb_tm"])
        p.mm(lambda e: e.transpose(pb[0:8, 0:128], comb_tm[:, 0:8], ident[:, :]),
             reads=["comb_tm", "ident_f32"], writes=["pb6"])
        p.op("vector", lambda e, tc=tc: e.tensor_copy(comb_fm[0:8, tc * 128:(tc + 1) * 128], pb[0:8, 0:128]),
             reads=["pb6"], writes=["comb_fm"])


def emit_ffn_expert(ctx, xln, xln_r, resid, resid_r, T_TILE, tile0, w1_d, w3_d, w2_d,
                    wtag, n_hc, comb_sel=None, uid=""):
    p = ctx.p
    NT = cdiv(T_TILE, 512)
    w1s = [ctx.b[f"w1s{j}"] for j in range(2)]
    w3s = [ctx.b[f"w3s{j}"] for j in range(2)]
    w2s = [ctx.b[f"w2s{j}"] for j in range(2)]
    tsil = ctx.b["tsil"]
    pbanks = ["pb0", "pb1", "pb2", "pb3"]
    xr8 = [xln_r + str(c) for c in range(8)]
    if comb_sel is not None:
        sel = ctx.b["sel8_sb"]
        bc = ctx.b["comb_bc"]
        for nt in range(NT):
            w = min(512, T_TILE - nt * 512)
            p.mm(lambda e, nt=nt, w=w: e.matmul(ctx.b["pb4"][:, 0:w], sel[:, comb_sel * 128:(comb_sel + 1) * 128],
                                                ctx.b["comb_fm"][0:8, nt * 512:nt * 512 + w], start=True, stop=True),
                 reads=["comb_fm", "sel8_sb"], writes=["pb4"])
            p.op("vector", lambda e, nt=nt, w=w: e.tensor_copy(bc[:, nt * 512:nt * 512 + w], ctx.b["pb4"][:, 0:w]),
                 reads=["pb4"], writes=["comb_bc"])

    for hc in range(n_hc):
        sl = hc % 2
        p.dma("gpsimd", w1s[sl][:, :], w1_d[hc], writes=[f"w1s{sl}"], tag=f"{wtag}w1{hc}")
        p.dma("gpsimd", w3s[sl][:, :], w3_d[hc], writes=[f"w3s{sl}"], tag=f"{wtag}w3{hc}")
        for nt in range(NT):
            t0 = tile0 + nt * 512
            w = min(512, T_TILE - nt * 512)
            pa = pbanks[(hc * NT + nt) % 2]
            pbk = pbanks[2 + (hc * NT + nt) % 2]
            def mm_a(e, sl=sl, t0=t0, w=w, pa=pa):
                r = None
                for fi in range(8):
                    r = e.matmul(ctx.b[pa][:, 0:w], w1s[sl][:, fi * 128:(fi + 1) * 128],
                                 xln[fi][:, t0:t0 + w], start=fi == 0, stop=fi == 7)
                return r
            p.mm(mm_a, reads=[f"w1s{sl}"] + xr8, writes=[pa])
            def mm_b(e, sl=sl, t0=t0, w=w, pbk=pbk):
                r = None
                for fi in range(8):
                    r = e.matmul(ctx.b[pbk][:, 0:w], w3s[sl][:, fi * 128:(fi + 1) * 128],
                                 xln[fi][:, t0:t0 + w], start=fi == 0, stop=fi == 7)
                return r
            p.mm(mm_b, reads=[f"w3s{sl}"] + xr8, writes=[pbk])
            p.op("scalar", lambda e, pa=pa, nt=nt, w=w: e.activation(
                tsil[:, nt * 512:nt * 512 + w], ctx.b[pa][:, 0:w], AF.Silu),
                reads=[pa], writes=["tsil"])
            p.op("vector", lambda e, pbk=pbk, hc=hc, nt=nt, w=w: e.tensor_tensor(
                ctx.b[f"hid{hc}"][:, nt * 512:nt * 512 + w], tsil[:, nt * 512:nt * 512 + w],
                ctx.b[pbk][:, 0:w], op=ALU.mult),
                reads=["tsil", pbk], writes=[f"hid{hc}"])

    for fo in range(8):
        sl = fo % 2
        p.dma("gpsimd", w2s[sl][:, 0:n_hc * 128], w2_d[fo], writes=[f"w2s{sl}"], tag=f"{wtag}w2{fo}")
        for nt in range(NT):
            t0l = nt * 512
            w = min(512, T_TILE - t0l)
            pa = pbanks[(fo * NT + nt) % 4]
            def mm_y(e, sl=sl, t0l=t0l, w=w, pa=pa):
                r = None
                for hc in range(n_hc):
                    r = e.matmul(ctx.b[pa][:, 0:w], w2s[sl][:, hc * 128:(hc + 1) * 128],
                                 ctx.b[f"hid{hc}"][:, t0l:t0l + w], start=hc == 0, stop=hc == n_hc - 1)
                return r
            p.mm(mm_y, reads=[f"w2s{sl}"] + [f"hid{hc}" for hc in range(n_hc)], writes=[pa])
            if comb_sel is not None:
                p.op("vector", lambda e, pa=pa, t0l=t0l, w=w: e.tensor_tensor(
                    ctx.b["ytmp"][:, 0:w], ctx.b[pa][:, 0:w], ctx.b["comb_bc"][:, t0l:t0l + w], op=ALU.mult),
                    reads=[pa, "comb_bc"], writes=["ytmp"])
                p.op("vector", lambda e, fo=fo, t0l=t0l, w=w: e.tensor_tensor(
                    resid[fo][:, tile0 + t0l:tile0 + t0l + w], resid[fo][:, tile0 + t0l:tile0 + t0l + w],
                    ctx.b["ytmp"][:, 0:w], op=ALU.add),
                    reads=["ytmp", resid_r + str(fo)], writes=[resid_r + str(fo)])
            else:
                p.op("vector", lambda e, pa=pa, fo=fo, t0l=t0l, w=w: e.tensor_tensor(
                    resid[fo][:, tile0 + t0l:tile0 + t0l + w], resid[fo][:, tile0 + t0l:tile0 + t0l + w],
                    ctx.b[pa][:, 0:w], op=ALU.add),
                    reads=[pa, resid_r + str(fo)], writes=[resid_r + str(fo)])


def alloc_moe_scratch(ctx, T_TILE, n_hc_max=22):
    for j in range(2):
        ctx.sb(f"w1s{j}", [128, 8 * 128], BF16)
        ctx.sb(f"w3s{j}", [128, 8 * 128], BF16)
        ctx.sb(f"w2s{j}", [128, n_hc_max * 128], BF16)
    ctx.sb("tsil", [128, T_TILE], BF16)
    for hc in range(n_hc_max):
        ctx.sb(f"hid{hc}", [128, T_TILE], BF16)
    ctx.sb("comb_tm", [128, 8], F32)
    ctx.sb("gate_m", [128, 4], F32)
    ctx.sb("gate_exp", [128, 8], F32)
    ctx.sb("comb_fm", [8, T_TILE], BF16)
    ctx.sb("comb_bc", [128, T_TILE], F32)
    ctx.sb("ytmp", [128, 512], F32)
    ctx.sb("gate_sb", [128, 8 * 8], BF16)
    ctx.sb("sel8_sb", [8, 8 * 128], BF16)
    ctx.sb("ident_f32", [128, 128], F32)


def emit_l0ffn(ctx, resid, resid_r, xln, xln_r, T, T_TILE, w1_ap, w3_ap, w2_ap,
               sw1_ap, sw3_ap, sw2_ap):
    p = ctx.p
    n_tiles = T // T_TILE
    for t in range(n_tiles):
        tile0 = t * T_TILE
        emit_gate_comb(ctx, xln, xln_r, ctx.b["gate_sb"], T_TILE, tile0)
        for e in range(8):
            emit_ffn_expert(ctx, xln, xln_r, resid, resid_r, T_TILE, tile0,
                            [w1_ap[e, hc] for hc in range(22)],
                            [w3_ap[e, hc] for hc in range(22)],
                            [w2_ap[e, fo] for fo in range(8)],
                            wtag=f"e{e}t{t}", n_hc=22, comb_sel=e)
        emit_ffn_expert(ctx, xln, xln_r, resid, resid_r, T_TILE, tile0,
                        [sw1_ap[hc] for hc in range(12)],
                        [sw3_ap[hc] for hc in range(12)],
                        [sw2_ap[fo] for fo in range(8)],
                        wtag=f"sht{t}", n_hc=12, comb_sel=None)


def prelay_gate(gate_w):
    return np.ascontiguousarray(gate_w.T.reshape(8, 128, 8).transpose(1, 0, 2).reshape(128, 64))


def sel8_np():
    s = np.zeros((8, 8 * 128), np.float32)
    for e in range(8):
        s[e, e * 128:(e + 1) * 128] = 1.0
    import ml_dtypes
    return s.astype(ml_dtypes.bfloat16)



def alloc_l1attn_scratch(ctx):
    ctx.sb("va0", [128, 1024], BF16)
    ctx.sb("va1", [128, 1024], BF16)
    ctx.sb("qb_sb", [128, 8], F32)
    ctx.sb("kb_sb", [128, 8], F32)
    ctx.sb("vb_sb", [128, 8], F32)
    ctx.sb("ob_sb", [128, 8], F32)
    ctx.sb("grep_w_sb", [8, 64], F32)
    ctx.sb("grep_b_sb", [8, 1], F32)
    ctx.sb("grep_a_sb", [16, 1], F32)
    ctx.sb("grep_lhsT", [128, 4], BF16)
    ctx.sb("grepsum", [64, 2], BF16)
    ctx.sb("grep_b4", [4, 1], F32)
    ctx.sb("a4_all", [4, 8], F32)
    ctx.sb("halfsel", [8, 2], F32)
    ctx.sb("halfsel2", [8, 4], F32)
    ctx.sb("perm24", [2, 4], F32)
    ctx.sb("shift24", [4, 4], BF16)
    ctx.sb("asm_sb", [4, 8 * 16], BF16)
    ctx.sb("eac_sb", [16, 4 * 8], F32)
    ctx.sb("ec_sb", [128, 8 * 16], BF16)
    ctx.sb("fc_sb", [16, 8 * 128], BF16)
    ctx.sb("posb_sb", [16, 4], F32)
    ctx.sb("gab", [4, 512], F32)
    ctx.sb("gab_bf", [4, 512], BF16)
    ctx.sb("gt3", [4, 512], F32)
    ctx.sb("gt3_bf", [4, 512], BF16)
    ctx.sb("ga1_all", [16, 1024], F32)
    ctx.sb("prod_bf", [128, 512], BF16)
    ctx.sb("sc00", [16, 512], F32)
    ctx.sb("sc01", [16, 512], F32)
    ctx.sb("sc10", [16, 512], F32)
    ctx.sb("sc11", [16, 512], F32)
    ctx.sb("scm", [16, 512], F32)
    ctx.sb("pn_bf", [16, 4 * 512], BF16)
    ctx.sb("att_tmp", [128, 512], F32)
    ctx.sb("att_tmp2", [128, 512], F32)


def emit_l1attn_init(ctx, grep_w_ap, grep_b_ap, grep_a_ap, posb_ap, qb_ap, kb_ap,
                     vb_ap, ob_ap, shift24_ap, asm_ap, eac_ap, ec_ap, fc_ap,
                     halfsel_ap, halfsel2_ap, perm24_ap):
    p = ctx.p
    b = ctx.b
    for name, ap in (("grep_w_sb", grep_w_ap), ("grep_b_sb", grep_b_ap),
                     ("grep_a_sb", grep_a_ap), ("posb_sb", posb_ap),
                     ("qb_sb", qb_ap), ("kb_sb", kb_ap), ("vb_sb", vb_ap),
                     ("ob_sb", ob_ap), ("shift24", shift24_ap), ("asm_sb", asm_ap),
                     ("eac_sb", eac_ap), ("ec_sb", ec_ap), ("fc_sb", fc_ap),
                     ("halfsel", halfsel_ap), ("halfsel2", halfsel2_ap),
                     ("perm24", perm24_ap)):
        p.dma("sync", b[name][:, :], ap, writes=[name])
    p.op("vector", lambda e: e.tensor_scalar_mul(b["qb_sb"][:, :], b["qb_sb"][:, :], 0.125),
         reads=["qb_sb"], writes=["qb_sb"])
    p.mm(lambda e: e.matmul(ctx.b["pb4"][0:64, 0:2], b["grep_w_sb"][:, :], b["halfsel"][:, :],
                            start=True, stop=True),
         reads=["grep_w_sb", "halfsel"], writes=["pb4"])
    p.op("vector", lambda e: e.tensor_copy(b["grepsum"][:, :], ctx.b["pb4"][0:64, 0:2]),
         reads=["pb4"], writes=["grepsum"])
    p.op("vector", lambda e: e.memset(b["grep_lhsT"][:, :], 0.0), writes=["grep_lhsT"])
    p.op("vector", lambda e: e.tensor_copy(b["grep_lhsT"][0:64, 0:2], b["grepsum"][:, :]),
         reads=["grepsum"], writes=["grep_lhsT"])
    p.op("vector", lambda e: e.tensor_copy(b["grep_lhsT"][64:128, 2:4], b["grepsum"][:, :]),
         reads=["grepsum"], writes=["grep_lhsT"])
    p.mm(lambda e: e.matmul(ctx.b["pb4"][0:2, 0:1], b["halfsel"][:, :], b["grep_b_sb"][:, :],
                            start=True, stop=True),
         reads=["halfsel", "grep_b_sb"], writes=["pb4"])
    p.op("vector", lambda e: e.tensor_copy(b["gab"][0:2, 0:1], ctx.b["pb4"][0:2, 0:1]),
         reads=["pb4"], writes=["gab"])
    p.mm(lambda e: e.matmul(ctx.b["pb4"][0:4, 0:1], b["perm24"][:, :], b["gab"][0:2, 0:1],
                            start=True, stop=True),
         reads=["perm24", "gab"], writes=["pb4"])
    p.op("vector", lambda e: e.tensor_copy(b["grep_b4"][:, :], ctx.b["pb4"][0:4, 0:1]),
         reads=["pb4"], writes=["grep_b4"])
    for c in range(8):
        p.mm(lambda e, c=c: e.matmul(ctx.b["pb4"][0:4, 0:1], b["eac_sb"][:, c * 4:(c + 1) * 4],
                                     b["grep_a_sb"][:, :], start=True, stop=True),
             reads=["eac_sb", "grep_a_sb"], writes=["pb4"])
        p.op("vector", lambda e, c=c: e.tensor_copy(b["a4_all"][:, c:c + 1], ctx.b["pb4"][0:4, 0:1]),
             reads=["pb4"], writes=["a4_all"])


def emit_l1attn(ctx, resid, resid_r, xln, xln_r, T, qw_ap, kw_ap, vw_ap, ow_ap):
    p = ctx.p
    b = ctx.b
    HB2 = T // 2
    q_t, qn = [b[f"hid{c}"] for c in range(8)], [f"hid{c}" for c in range(8)]
    k_t, kn = [b[f"hid{8 + c}"] for c in range(8)], [f"hid{8 + c}" for c in range(8)]
    v_t = [b[f"hid{16 + c}"] for c in range(6)] + [b["va0"], b["va1"]]
    vn = [f"hid{16 + c}" for c in range(6)] + ["va0", "va1"]
    NST = cdiv(HB2, 512)
    xr8 = [xln_r + str(c) for c in range(8)]
    pbanks = ["pb0", "pb1", "pb2", "pb3"]
    pmap = {(0, 0): "pb0", (0, 1): "pb1", (1, 0): "pb2", (1, 1): "pb3"}

    for st in range(NST):
        s0 = st * 512
        cw = min(512, HB2 - s0)
        cols = [(s0, cw), (HB2 + s0, cw)]
        for pi, (w_ap, dsts, dns, bias, bsc) in enumerate((
                (qw_ap, q_t, qn, "qb_sb", 0.125),
                (kw_ap, k_t, kn, "kb_sb", 1.0),
                (vw_ap, v_t, vn, "vb_sb", 1.0))):
            for fo in range(8):
                sl = fo % 2
                wreg = f"w1s{sl}"
                p.dma("gpsimd", b[wreg][:, :], w_ap[fo], writes=[wreg], tag=f"l1a{pi}{fo}")
                for hf, (c0, _) in enumerate(cols):
                    pa = pbanks[(fo * 2 + hf) % 4]
                    def mm_p(e, c0=c0, cw=cw, pa=pa, wreg=wreg):
                        r = None
                        for fi in range(8):
                            r = e.matmul(b[pa][:, 0:cw], b[wreg][:, fi * 128:(fi + 1) * 128],
                                         xln[fi][:, c0:c0 + cw], start=fi == 0, stop=fi == 7)
                        return r
                    p.mm(mm_p, reads=[wreg] + xr8, writes=[pa])
                    p.op("scalar", lambda e, pa=pa, hf=hf, cw=cw, fo=fo, dsts=dsts, bias=bias, bsc=bsc:
                         e.activation(dsts[fo][:, hf * 512:hf * 512 + cw], b[pa][:, 0:cw], AF.Identity,
                                      bias=b[bias][:, fo:fo + 1], scale=bsc),
                         reads=[pa, bias], writes=[dns[fo]])

        for hf, (c0, _) in enumerate(cols):
            for c in range(8):
                p.mm(lambda e, c=c, c0=c0, cw=cw: e.matmul(
                    b["pb4"][0:4, 0:cw], b["grep_lhsT"][:, 0:4], xln[c][:, c0:c0 + cw],
                    start=True, stop=True),
                    reads=["grep_lhsT", xln_r + str(c)], writes=["pb4"])
                p.op("scalar", lambda e, cw=cw: e.activation(
                    b["gab"][0:4, 0:cw], b["pb4"][0:4, 0:cw], AF.Sigmoid,
                    bias=b["grep_b4"][0:4, 0:1]),
                    reads=["pb4", "grep_b4"], writes=["gab"])
                p.op("vector", lambda e, cw=cw: e.tensor_copy(b["gab_bf"][0:4, 0:cw], b["gab"][0:4, 0:cw]),
                     reads=["gab"], writes=["gab_bf"])
                p.mm(lambda e, cw=cw: e.matmul(
                    b["pb5"][0:4, 0:cw], b["shift24"][0:4, 0:4], b["gab_bf"][0:4, 0:cw],
                    start=True, stop=True),
                    reads=["shift24", "gab_bf"], writes=["pb5"])
                p.op("vector", lambda e, c=c, cw=cw: e.tensor_scalar(
                    b["gt3"][0:4, 0:cw], b["pb5"][0:4, 0:cw], b["a4_all"][0:4, c:c + 1], -1.0,
                    op0=ALU.mult, op1=ALU.add),
                    reads=["pb5", "a4_all"], writes=["gt3"])
                p.op("vector", lambda e, cw=cw: e.tensor_tensor(
                    b["gt3"][0:4, 0:cw], b["gt3"][0:4, 0:cw], b["gab"][0:4, 0:cw], op=ALU.mult),
                    reads=["gt3", "gab"], writes=["gt3"])
                p.op("vector", lambda e, cw=cw: e.tensor_scalar(
                    b["gt3_bf"][0:4, 0:cw], b["gt3"][0:4, 0:cw], 2.0, None, op0=ALU.add),
                    reads=["gt3"], writes=["gt3_bf"])
                p.mm(lambda e, c=c, cw=cw: e.matmul(
                    b["pb6"][0:16, 0:cw], b["asm_sb"][0:4, c * 16:(c + 1) * 16], b["gt3_bf"][0:4, 0:cw],
                    start=c == 0, stop=c == 7),
                    reads=["asm_sb", "gt3_bf"], writes=["pb6"])
            p.op("vector", lambda e, hf=hf, cw=cw: e.tensor_copy(
                b["ga1_all"][0:16, hf * 512:hf * 512 + cw], b["pb6"][0:16, 0:cw]),
                reads=["pb6"], writes=["ga1_all"])

        for i in range(2):
            for j in range(2):
                pij = pmap[(i, j)]
                for c in range(8):
                    p.op("vector", lambda e, c=c, i=i, j=j, cw=cw: e.tensor_tensor(
                        b["prod_bf"][:, 0:cw], q_t[c][:, i * 512:i * 512 + cw],
                        k_t[c][:, j * 512:j * 512 + cw], op=ALU.mult),
                        reads=[qn[c], kn[c]], writes=["prod_bf"])
                    p.mm(lambda e, c=c, cw=cw, pij=pij: e.matmul(
                        b[pij][0:16, 0:cw], b["ec_sb"][:, c * 16:(c + 1) * 16], b["prod_bf"][:, 0:cw],
                        start=c == 0, stop=c == 7),
                        reads=["ec_sb", "prod_bf"], writes=[pij])
                p.op("vector", lambda e, i=i, j=j, cw=cw, pij=pij: e.scalar_tensor_tensor(
                    b[f"sc{i}{j}"][0:16, 0:cw], b["ga1_all"][0:16, i * 512:i * 512 + cw],
                    b["posb_sb"][0:16, 2 * i + j:2 * i + j + 1], b[pij][0:16, 0:cw],
                    op0=ALU.mult, op1=ALU.add),
                    reads=["ga1_all", "posb_sb", pij], writes=[f"sc{i}{j}"])
        for i in range(2):
            p.op("vector", lambda e, i=i, cw=cw: e.tensor_tensor(
                b["scm"][0:16, 0:cw], b[f"sc{i}0"][0:16, 0:cw], b[f"sc{i}1"][0:16, 0:cw], op=ALU.max),
                reads=[f"sc{i}0", f"sc{i}1"], writes=["scm"])
            for j in range(2):
                p.op("vector", lambda e, i=i, j=j, cw=cw: e.tensor_tensor(
                    b[f"sc{i}{j}"][0:16, 0:cw], b[f"sc{i}{j}"][0:16, 0:cw], b["scm"][0:16, 0:cw],
                    op=ALU.subtract),
                    reads=[f"sc{i}{j}", "scm"], writes=[f"sc{i}{j}"])
                p.op("scalar", lambda e, i=i, j=j, cw=cw: e.activation(
                    b[f"sc{i}{j}"][0:16, 0:cw], b[f"sc{i}{j}"][0:16, 0:cw], AF.Exp),
                    reads=[f"sc{i}{j}"], writes=[f"sc{i}{j}"])
            p.op("vector", lambda e, i=i, cw=cw: e.tensor_tensor(
                b["scm"][0:16, 0:cw], b[f"sc{i}0"][0:16, 0:cw], b[f"sc{i}1"][0:16, 0:cw], op=ALU.add),
                reads=[f"sc{i}0", f"sc{i}1"], writes=["scm"])
            p.op("vector", lambda e, cw=cw: e.reciprocal(b["scm"][0:16, 0:cw], b["scm"][0:16, 0:cw]),
                 reads=["scm"], writes=["scm"])
            for j in range(2):
                p.op("vector", lambda e, i=i, j=j, cw=cw: e.tensor_tensor(
                    b["pn_bf"][0:16, (2 * i + j) * 512:(2 * i + j) * 512 + cw],
                    b[f"sc{i}{j}"][0:16, 0:cw], b["scm"][0:16, 0:cw], op=ALU.mult),
                    reads=[f"sc{i}{j}", "scm"], writes=["pn_bf"])
        for c in range(8):
            for i in range(2):
                c0 = cols[i][0]
                p.mm(lambda e, c=c, i=i, cw=cw: e.matmul(
                    b["pb4"][:, 0:cw], b["fc_sb"][0:16, c * 128:(c + 1) * 128],
                    b["pn_bf"][0:16, (2 * i + 0) * 512:(2 * i + 0) * 512 + cw],
                    start=True, stop=True),
                    reads=["fc_sb", "pn_bf"], writes=["pb4"])
                p.op("vector", lambda e, c=c, cw=cw: e.tensor_tensor(
                    b["att_tmp"][:, 0:cw], b["pb4"][:, 0:cw], v_t[c][:, 0:cw], op=ALU.mult),
                    reads=["pb4", vn[c]], writes=["att_tmp"])
                p.mm(lambda e, c=c, i=i, cw=cw: e.matmul(
                    b["pb5"][:, 0:cw], b["fc_sb"][0:16, c * 128:(c + 1) * 128],
                    b["pn_bf"][0:16, (2 * i + 1) * 512:(2 * i + 1) * 512 + cw],
                    start=True, stop=True),
                    reads=["fc_sb", "pn_bf"], writes=["pb5"])
                p.op("vector", lambda e, c=c, cw=cw: e.tensor_tensor(
                    b["att_tmp2"][:, 0:cw], b["pb5"][:, 0:cw], v_t[c][:, 512:512 + cw], op=ALU.mult),
                    reads=["pb5", vn[c]], writes=["att_tmp2"])
                p.op("vector", lambda e, c=c, c0=c0, cw=cw: e.tensor_tensor(
                    xln[c][:, c0:c0 + cw], b["att_tmp"][:, 0:cw], b["att_tmp2"][:, 0:cw], op=ALU.add),
                    reads=["att_tmp", "att_tmp2"], writes=[xln_r + str(c)])
        for fo in range(8):
            sl = fo % 2
            wreg = f"w1s{sl}"
            p.dma("gpsimd", b[wreg][:, :], ow_ap[fo], writes=[wreg], tag=f"l1ao{fo}")
            for hf, (c0, _) in enumerate(cols):
                pa = pbanks[(fo * 2 + hf) % 4]
                def mm_o(e, c0=c0, cw=cw, pa=pa, wreg=wreg):
                    r = None
                    for fi in range(8):
                        r = e.matmul(b[pa][:, 0:cw], b[wreg][:, fi * 128:(fi + 1) * 128],
                                     xln[fi][:, c0:c0 + cw], start=fi == 0, stop=fi == 7)
                    return r
                p.mm(mm_o, reads=[wreg] + xr8, writes=[pa])
                p.op("vector", lambda e, pa=pa, fo=fo, c0=c0, cw=cw: e.scalar_tensor_tensor(
                    resid[fo][:, c0:c0 + cw], b[pa][:, 0:cw], b["ob_sb"][:, fo:fo + 1],
                    resid[fo][:, c0:c0 + cw], op0=ALU.add, op1=ALU.add),
                    reads=[pa, "ob_sb", resid_r + str(fo)], writes=[resid_r + str(fo)])


def l1attn_consts():
    import ml_dtypes
    shift24 = np.zeros((4, 4), np.float32); shift24[1, 0] = 1; shift24[3, 2] = 1
    asm = np.zeros((8, 4, 16), np.float32)
    for c in range(8):
        asm[c, 0, 2 * c] = 1; asm[c, 2, 2 * c + 1] = 1
    eac = np.zeros((8, 16, 4), np.float32)
    for c in range(8):
        eac[c, 2 * c, 0] = 1; eac[c, 2 * c + 1, 2] = 1
    ec = np.zeros((8, 128, 16), np.float32)
    for c in range(8):
        ec[c, 0:64, 2 * c] = 1; ec[c, 64:128, 2 * c + 1] = 1
    fc = np.zeros((8, 16, 128), np.float32)
    for c in range(8):
        fc[c, 2 * c, 0:64] = 1; fc[c, 2 * c + 1, 64:128] = 1
    halfsel = np.zeros((8, 2), np.float32); halfsel[0:4, 0] = 1; halfsel[4:8, 1] = 1
    halfsel2 = np.zeros((8, 4), np.float32)
    perm24 = np.zeros((2, 4), np.float32); perm24[0, 0] = 1; perm24[1, 1] = 1; perm24[0, 2] = 1; perm24[1, 3] = 1
    bf = ml_dtypes.bfloat16
    return dict(
        l1_shift24=shift24.astype(bf),
        l1_asm=np.ascontiguousarray(asm.transpose(1, 0, 2).reshape(4, 128)).astype(bf),
        l1_eac=np.ascontiguousarray(eac.transpose(1, 0, 2).reshape(16, 32)).astype(np.float32),
        l1_ec=np.ascontiguousarray(ec.transpose(1, 0, 2).reshape(128, 128)).astype(bf),
        l1_fc=np.ascontiguousarray(fc.transpose(1, 0, 2).reshape(16, 1024)).astype(bf),
        l1_halfsel=halfsel, l1_halfsel2=halfsel2, l1_perm24=perm24)


GLA_SUP = 256
GLA_C = 128


def alloc_gla_scratch(ctx):
    ctx.sb("vtm", [128, 2048], BF16)
    ctx.sb("vfm_t", [128, GLA_SUP], BF16)
    ctx.sb("Lfm", [128, 8 * GLA_SUP], F32)
    ctx.sb("Lcarry", [128, 8], F32)
    ctx.sb("S_all", [128, 512], F32)
    ctx.sb("S_init", [128, 1024], F32)
    ctx.sb("pf", [64, GLA_SUP], BF16)
    ctx.sb("lgs_t", [128, GLA_SUP], F32)
    ctx.sb("lgs_t2", [128, GLA_SUP], F32)
    ctx.sb("uraw", [128, GLA_SUP], F32)
    ctx.sb("e1q", [128, GLA_C], BF16)
    ctx.sb("e2", [128, GLA_C], F32)
    ctx.sb("qpp", [128, GLA_C], BF16)
    ctx.sb("kpp", [128, GLA_C], BF16)
    ctx.sb("k2c", [128, GLA_C], BF16)
    ctx.sb("k2tm", [128, GLA_C], BF16)
    ctx.sb("atmA", [128, GLA_C], BF16)
    ctx.sb("atmB", [128, GLA_C], BF16)
    ctx.sb("utmask", [128, 128], BF16)
    ctx.sb("ident_bf", [128, 128], BF16)
    ctx.sb("gcol", [128, 8], F32)
    ctx.sb("s0x", [128, 128], BF16)
    ctx.sb("og_f", [128, GLA_C], F32)
    ctx.sb("og2", [128, GLA_C], BF16)
    ctx.sb("rstd16", [16, GLA_C], F32)
    ctx.sb("rstd16b", [16, GLA_C], BF16)
    ctx.sb("nw_col", [128, 8], F32)
    ctx.sb("ones_col_f32", [128, 1], F32)
    ctx.sb("lnq_col", [128, 1], F32)
    ctx.sb("zsum", [128, 8], F32)
    ctx.sb("gsum_st", [128, 520], F32)
    ctx.sb("wf1s", [128, 512], BF16)
    ctx.sb("wg1s", [128, 512], BF16)
    ctx.sb("wf2s", [64, 1024], BF16)
    ctx.sb("wg2s", [64, 1024], BF16)
    p = ctx.p
    p.op("vector", lambda e: e.memset(ctx.b["ones_col_f32"][:], 1.0), writes=["ones_col_f32"])
    p.op("vector", lambda e: e.memset(ctx.b["lnq_col"][:], float(np.log(0.125))), writes=["lnq_col"])
    p.op("vector", lambda e: e.memset(ctx.b["s0x"][:, :], 0.0), writes=["s0x"])
    p.op("gpsimd", lambda e: e.memset(ctx.b["ident_bf"][:], 1.0), writes=["ident_bf"])
    p.op("gpsimd", lambda e: e.affine_select(
        out=ctx.b["ident_bf"][:], in_=ctx.b["ident_bf"][:], pattern=[[1, 128]],
        compare_op=ALU.is_equal, fill=0.0, channel_multiplier=-1),
        reads=["ident_bf"], writes=["ident_bf"])
    p.op("gpsimd", lambda e: e.memset(ctx.b["utmask"][:], 1.0), writes=["utmask"])
    p.op("gpsimd", lambda e: e.affine_select(
        out=ctx.b["utmask"][:], in_=ctx.b["utmask"][:], pattern=[[1, 128]],
        compare_op=ALU.is_ge, fill=0.0, channel_multiplier=-1),
        reads=["utmask"], writes=["utmask"])


def emit_gla_pass(ctx, xln, xln_r, resid, resid_r, passB,
                  wq_ap, wk_ap, wv_ap, wo_ap, wf1_ap, wf2_ap, wg1_ap, wg2_ap,
                  nw_ap, gsum_out_ap=None, T=2048):
    p = ctx.p
    b = ctx.b
    xr8 = [xln_r + str(c) for c in range(8)]

    p.dma("gpsimd", b["wf1s"][:, :], wf1_ap, writes=["wf1s"], tag="wf1")
    p.dma("gpsimd", b["wf2s"][:, :], wf2_ap, writes=["wf2s"], tag="wf2")
    if passB:
        p.dma("gpsimd", b["wg1s"][:, :], wg1_ap, writes=["wg1s"], tag="wg1")
        p.dma("gpsimd", b["wg2s"][:, :], wg2_ap, writes=["wg2s"], tag="wg2")
        p.dma("sync", b["nw_col"][:, :], nw_ap, writes=["nw_col"])

    for bh in range(2):
        if passB:
            p.op("vector", lambda e, bh=bh: e.tensor_copy(b["S_all"][:, :], b["S_init"][:, bh * 512:(bh + 1) * 512]),
                 reads=["S_init"], writes=["S_all"])
        else:
            p.op("vector", lambda e: e.memset(b["S_all"][:, :], 0.0), writes=["S_all"])
        p.op("vector", lambda e: e.memset(b["Lcarry"][:, :], 0.0), writes=["Lcarry"])

        half = T // 2
        for sup4 in range(half // GLA_SUP):
            t0 = bh * half + sup4 * GLA_SUP
            projs = [("k", wk_ap, AF.Silu, 512)]
            if passB:
                projs = [("q", wq_ap, AF.Silu, 0)] + projs
            for pname, w_ap, act, dstoff in projs:
                for fo in range(8):
                    sl = fo % 2
                    wreg = f"w1s{sl}"
                    p.dma("gpsimd", b[wreg][:, :], w_ap[fo], writes=[wreg], tag=f"gla{pname}{fo}")
                    pa = f"pb{fo % 2}"
                    def mm_p(e, wreg=wreg, t0=t0, pa=pa):
                        r = None
                        for fi in range(8):
                            r = e.matmul(b[pa][:, 0:GLA_SUP], b[wreg][:, fi * 128:(fi + 1) * 128],
                                         xln[fi][:, t0:t0 + GLA_SUP], start=fi == 0, stop=fi == 7)
                        return r
                    p.mm(mm_p, reads=[wreg] + xr8, writes=[pa])
                    p.op("scalar", lambda e, pa=pa, fo=fo, act=act, dstoff=dstoff: e.activation(
                        b[f"hid{fo}"][:, dstoff:dstoff + GLA_SUP], b[pa][:, 0:GLA_SUP], act),
                        reads=[pa], writes=[f"hid{fo}"])
            for fo in range(8):
                sl = fo % 2
                wreg = f"w1s{sl}"
                p.dma("gpsimd", b[wreg][:, :], wv_ap[fo], writes=[wreg], tag=f"glav{fo}")
                pa = f"pb{fo % 2}"
                def mm_v(e, wreg=wreg, t0=t0, pa=pa):
                    r = None
                    for fi in range(8):
                        r = e.matmul(b[pa][:, 0:GLA_SUP], b[wreg][:, fi * 128:(fi + 1) * 128],
                                     xln[fi][:, t0:t0 + GLA_SUP], start=fi == 0, stop=fi == 7)
                    return r
                p.mm(mm_v, reads=[wreg] + xr8, writes=[pa])
                p.op("vector", lambda e, pa=pa: e.tensor_copy(b["vfm_t"][:, 0:GLA_SUP], b[pa][:, 0:GLA_SUP]),
                     reads=[pa], writes=["vfm_t"])
                for ci in range(2):
                    p.mm(lambda e, ci=ci: e.transpose(b["pbt"][:, 0:128], b["vfm_t"][:, ci * 128:(ci + 1) * 128],
                                                      b["ident_bf"][:, :]),
                         reads=["vfm_t", "ident_bf"], writes=["pbt"])
                    p.op("vector", lambda e, ci=ci, fo=fo: e.tensor_copy(
                        b["vtm"][:, ci * 1024 + fo * 128:ci * 1024 + fo * 128 + 128], b["pbt"][:, 0:128]),
                        reads=["pbt"], writes=["vtm"])
            if passB:
                def mm_pg(e, t0=t0):
                    r = None
                    for fi in range(8):
                        r = e.matmul(b["pb2"][0:64, 0:GLA_SUP], b["wg1s"][:, fi * 64:(fi + 1) * 64],
                                     xln[fi][:, t0:t0 + GLA_SUP], start=fi == 0, stop=fi == 7)
                    return r
                p.mm(mm_pg, reads=["wg1s"] + xr8, writes=["pb2"])
                p.op("vector", lambda e: e.tensor_copy(b["pf"][:, 0:GLA_SUP], b["pb2"][0:64, 0:GLA_SUP]),
                     reads=["pb2"], writes=["pf"])
                for fo in range(8):
                    p.mm(lambda e, fo=fo: e.matmul(b["pb3"][:, 0:GLA_SUP], b["wg2s"][:, fo * 128:(fo + 1) * 128],
                                                   b["pf"][:, 0:GLA_SUP], start=True, stop=True),
                         reads=["wg2s", "pf"], writes=["pb3"])
                    p.op("scalar", lambda e, fo=fo: e.activation(
                        b[f"hid{8 + fo}"][:, 0:GLA_SUP], b["pb3"][:, 0:GLA_SUP], AF.Sigmoid),
                        reads=["pb3"], writes=[f"hid{8 + fo}"])
            def mm_pf(e, t0=t0):
                r = None
                for fi in range(8):
                    r = e.matmul(b["pb2"][0:64, 0:GLA_SUP], b["wf1s"][:, fi * 64:(fi + 1) * 64],
                                 xln[fi][:, t0:t0 + GLA_SUP], start=fi == 0, stop=fi == 7)
                return r
            p.mm(mm_pf, reads=["wf1s"] + xr8, writes=["pb2"])
            p.op("vector", lambda e: e.tensor_copy(b["pf"][:, 0:GLA_SUP], b["pb2"][0:64, 0:GLA_SUP]),
                 reads=["pb2"], writes=["pf"])
            for fo in range(8):
                p.mm(lambda e, fo=fo: e.matmul(b["pb3"][:, 0:GLA_SUP], b["wf2s"][:, fo * 128:(fo + 1) * 128],
                                               b["pf"][:, 0:GLA_SUP], start=True, stop=True),
                     reads=["wf2s", "pf"], writes=["pb3"])
                p.op("vector", lambda e: e.tensor_copy(b["uraw"][:, 0:GLA_SUP], b["pb3"][:, 0:GLA_SUP]),
                     reads=["pb3"], writes=["uraw"])
                p.op("scalar", lambda e: e.activation(b["lgs_t"][:, 0:GLA_SUP], b["uraw"][:, 0:GLA_SUP], AF.Abs),
                     reads=["uraw"], writes=["lgs_t"])
                p.op("scalar", lambda e: e.activation(b["lgs_t"][:, 0:GLA_SUP], b["lgs_t"][:, 0:GLA_SUP], AF.Exp, scale=-1.0),
                     reads=["lgs_t"], writes=["lgs_t"])
                p.op("scalar", lambda e: e.activation(b["lgs_t"][:, 0:GLA_SUP], b["lgs_t"][:, 0:GLA_SUP], AF.Ln,
                                                      bias=b["ones_col_f32"][:, 0:1]),
                     reads=["lgs_t", "ones_col_f32"], writes=["lgs_t"])
                p.op("vector", lambda e: e.tensor_scalar_min(b["lgs_t2"][:, 0:GLA_SUP], b["uraw"][:, 0:GLA_SUP], 0.0),
                     reads=["uraw"], writes=["lgs_t2"])
                p.op("vector", lambda e: e.tensor_tensor(b["lgs_t2"][:, 0:GLA_SUP], b["lgs_t2"][:, 0:GLA_SUP],
                                                         b["lgs_t"][:, 0:GLA_SUP], op=ALU.subtract),
                     reads=["lgs_t", "lgs_t2"], writes=["lgs_t2"])
                p.op("vector", lambda e, fo=fo: e.tensor_tensor_scan(
                    b["Lfm"][:, fo * GLA_SUP:(fo + 1) * GLA_SUP], b["lgs_t2"][:, 0:GLA_SUP],
                    b["lgs_t2"][:, 0:GLA_SUP], b["Lcarry"][:, fo:fo + 1], ALU.add, ALU.bypass),
                    reads=["lgs_t2", "Lcarry"], writes=["Lfm"])

            for ci in range(2):
                for fc in range(8):
                    Lsl = b["Lfm"][:, fc * GLA_SUP + ci * 128:fc * GLA_SUP + ci * 128 + 128]
                    m_col = b["Lfm"][:, fc * GLA_SUP + ci * 128 + 63:fc * GLA_SUP + ci * 128 + 64]
                    LC_col = b["Lfm"][:, fc * GLA_SUP + ci * 128 + 127:fc * GLA_SUP + ci * 128 + 128]
                    if ci == 1:
                        pLC_col = b["Lfm"][:, fc * GLA_SUP + 127:fc * GLA_SUP + 128]
                        pLC_reg = "Lfm"
                    else:
                        pLC_col = b["Lcarry"][:, fc:fc + 1]
                        pLC_reg = "Lcarry"
                    vsl_A = b["vtm"][:, ci * 1024 + fc * 128:ci * 1024 + fc * 128 + 64]
                    vsl_B = b["vtm"][:, ci * 1024 + fc * 128 + 64:ci * 1024 + fc * 128 + 128]
                    ksl = b[f"hid{fc}"][:, 512 + ci * 128:512 + ci * 128 + 128]
                    Ssl = b["S_all"][:, fc * 64:(fc + 1) * 64]

                    p.op("scalar", lambda e, Lsl=Lsl, m_col=m_col: e.activation(
                        b["e2"][:, 0:128], Lsl, AF.Exp, scale=-1.0, bias=m_col),
                        reads=["Lfm"], writes=["e2"])
                    p.op("vector", lambda e, ksl=ksl: e.tensor_tensor(
                        b["kpp"][:, 0:128], ksl, b["e2"][:, 0:128], op=ALU.mult),
                        reads=[f"hid{fc}", "e2"], writes=["kpp"])
                    p.op("vector", lambda e, LC_col=LC_col, m_col=m_col: e.tensor_tensor(
                        b["gcol"][:, 0:1], LC_col, m_col, op=ALU.subtract),
                        reads=["Lfm"], writes=["gcol"])
                    p.op("vector", lambda e, LC_col=LC_col, pLC_col=pLC_col: e.tensor_tensor(
                        b["gcol"][:, 1:2], LC_col, pLC_col, op=ALU.subtract),
                        reads=["Lfm", pLC_reg], writes=["gcol"])
                    if passB:
                        p.op("vector", lambda e, m_col=m_col, pLC_col=pLC_col: e.tensor_tensor(
                            b["gcol"][:, 2:3], m_col, pLC_col, op=ALU.subtract),
                            reads=["Lfm", pLC_reg], writes=["gcol"])
                        p.op("scalar", lambda e: e.activation(b["gcol"][:, 0:3], b["gcol"][:, 0:3], AF.Exp),
                             reads=["gcol"], writes=["gcol"])
                    else:
                        p.op("scalar", lambda e: e.activation(b["gcol"][:, 0:2], b["gcol"][:, 0:2], AF.Exp),
                             reads=["gcol"], writes=["gcol"])
                    p.op("vector", lambda e: e.tensor_scalar(
                        b["k2c"][:, 0:128], b["kpp"][:, 0:128], b["gcol"][:, 0:1], None, op0=ALU.mult),
                        reads=["kpp", "gcol"], writes=["k2c"])
                    p.mm(lambda e: e.transpose(b["pbt"][:, 0:128], b["k2c"][:, 0:128], b["ident_bf"][:, :]),
                         reads=["k2c", "ident_bf"], writes=["pbt"])
                    p.op("vector", lambda e: e.tensor_copy(b["k2tm"][:, 0:128], b["pbt"][:, 0:128]),
                         reads=["pbt"], writes=["k2tm"])

                    if passB:
                        p.op("vector", lambda e, m_col=m_col: e.tensor_tensor(
                            b["gcol"][:, 3:4], b["lnq_col"][:, 0:1], m_col, op=ALU.subtract),
                            reads=["lnq_col", "Lfm"], writes=["gcol"])
                        p.op("scalar", lambda e, Lsl=Lsl: e.activation(
                            b["e1q"][:, 0:128], Lsl, AF.Exp, bias=b["gcol"][:, 3:4]),
                            reads=["Lfm", "gcol"], writes=["e1q"])
                        qsl = b[f"hid{fc}"][:, ci * 128:ci * 128 + 128]
                        p.op("vector", lambda e, qsl=qsl: e.tensor_tensor(
                            b["qpp"][:, 0:128], qsl, b["e1q"][:, 0:128], op=ALU.mult),
                            reads=[f"hid{fc}", "e1q"], writes=["qpp"])
                        p.mm(lambda e: e.matmul(b["pb0"][:, 0:128], b["kpp"][0:64, 0:128], b["qpp"][0:64, 0:128],
                                                start=True, stop=True),
                             reads=["kpp", "qpp"], writes=["pb0"])
                        p.mm(lambda e: e.matmul(b["pb1"][:, 0:128], b["kpp"][64:128, 0:128], b["qpp"][64:128, 0:128],
                                                start=True, stop=True),
                             reads=["kpp", "qpp"], writes=["pb1"])
                        p.op("vector", lambda e: e.tensor_tensor(
                            b["atmA"][:, 0:128], b["pb0"][:, 0:128], b["utmask"][:, :], op=ALU.mult),
                            reads=["pb0", "utmask"], writes=["atmA"])
                        p.op("vector", lambda e: e.tensor_tensor(
                            b["atmB"][:, 0:128], b["pb1"][:, 0:128], b["utmask"][:, :], op=ALU.mult),
                            reads=["pb1", "utmask"], writes=["atmB"])
                        p.op("vector", lambda e, Ssl=Ssl: e.tensor_scalar(
                            b["s0x"][0:64, 0:64], Ssl[0:64, 0:64], b["gcol"][0:64, 2:3], None, op0=ALU.mult),
                            reads=["S_all", "gcol"], writes=["s0x"])
                        p.op("vector", lambda e, Ssl=Ssl: e.tensor_scalar(
                            b["s0x"][64:128, 64:128], Ssl[64:128, 0:64], b["gcol"][64:128, 2:3], None, op0=ALU.mult),
                            reads=["S_all", "gcol"], writes=["s0x"])
                        def mm_o(e, vsl_A=vsl_A, vsl_B=vsl_B):
                            e.matmul(b["pb2"][0:64, 0:128], vsl_A, b["atmA"][:, 0:128],
                                     start=True, stop=True)
                            return e.matmul(b["pb2"][64:128, 0:128], vsl_B, b["atmB"][:, 0:128],
                                            start=True, stop=True, tile_position=(0, 64))
                        p.mm(mm_o, reads=["vtm", "atmA", "atmB"], writes=["pb2"])
                        p.mm(lambda e: e.matmul(b["pb5"][:, 0:128], b["s0x"][:, :], b["qpp"][:, 0:128],
                                                start=True, stop=True),
                             reads=["s0x", "qpp"], writes=["pb5"])
                        gsl = b[f"hid{8 + fc}"][:, ci * 128:ci * 128 + 128]
                        p.op("vector", lambda e: e.tensor_copy(b["og_f"][:, 0:128], b["pb5"][:, 0:128]),
                             reads=["pb5"], writes=["og_f"])
                        p.op("vector", lambda e: e.scalar_tensor_tensor(
                            b["og_f"][:, 0:128], b["pb2"][:, 0:128], 1.0, b["og_f"][:, 0:128],
                            op0=ALU.mult, op1=ALU.add),
                            reads=["pb2", "og_f"], writes=["og_f"])
                        p.op("vector", lambda e, gsl=gsl: e.tensor_tensor(
                            b["og_f"][:, 0:128], b["og_f"][:, 0:128], gsl, op=ALU.mult),
                            reads=["og_f", f"hid{8 + fc}"], writes=["og_f"])
                        p.op("scalar", lambda e: e.activation(b["og2"][:, 0:128], b["og_f"][:, 0:128], AF.Square),
                             reads=["og_f"], writes=["og2"])
                        p.mm(lambda e, fc=fc: e.matmul(b["pb6"][0:16, 0:128], b["ec_sb"][:, fc * 16:(fc + 1) * 16],
                                                       b["og2"][:, 0:128], start=fc == 0, stop=fc == 7),
                             reads=["ec_sb", "og2"], writes=["pb6"])
                        p.op("vector", lambda e, fc=fc, ci=ci: e.tensor_copy(
                            b[f"hid{8 + fc}"][:, 512 + ci * 128:512 + ci * 128 + 128], b["og_f"][:, 0:128]),
                            reads=["og_f"], writes=[f"hid{8 + fc}"])
                    def mm_u(e, vsl_A=vsl_A, vsl_B=vsl_B):
                        e.matmul(b["pb3"][0:64, 0:64], b["k2tm"][:, 0:64], vsl_A, start=True, stop=True)
                        return e.matmul(b["pb4"][64:128, 0:64], b["k2tm"][:, 64:128], vsl_B,
                                        start=True, stop=True, tile_position=(0, 64))
                    p.mm(mm_u, reads=["k2tm", "vtm"], writes=["pb3", "pb4"])
                    p.op("vector", lambda e, Ssl=Ssl: e.scalar_tensor_tensor(
                        Ssl[0:64, 0:64], Ssl[0:64, 0:64], b["gcol"][0:64, 1:2], b["pb3"][0:64, 0:64],
                        op0=ALU.mult, op1=ALU.add),
                        reads=["S_all", "gcol", "pb3"], writes=["S_all"])
                    p.op("vector", lambda e, Ssl=Ssl: e.scalar_tensor_tensor(
                        Ssl[64:128, 0:64], Ssl[64:128, 0:64], b["gcol"][64:128, 1:2], b["pb4"][64:128, 0:64],
                        op0=ALU.mult, op1=ALU.add),
                        reads=["S_all", "gcol", "pb4"], writes=["S_all"])
                if passB:
                    p.op("scalar", lambda e: e.activation(b["rstd16"][0:16, 0:128], b["pb6"][0:16, 0:128],
                                                          AF.Sqrt, scale=1.0 / 64, bias=b["eps_col"][0:16, 0:1]),
                         reads=["pb6", "eps_col"], writes=["rstd16"])
                    p.op("vector", lambda e: e.reciprocal(b["rstd16"][0:16, 0:128], b["rstd16"][0:16, 0:128]),
                         reads=["rstd16"], writes=["rstd16"])
                    p.op("vector", lambda e: e.tensor_copy(b["rstd16b"][0:16, 0:128], b["rstd16"][0:16, 0:128]),
                         reads=["rstd16"], writes=["rstd16b"])
                    for fc in range(8):
                        p.mm(lambda e, fc=fc: e.matmul(b["pb5"][:, 0:128], b["fc_sb"][0:16, fc * 128:(fc + 1) * 128],
                                                       b["rstd16b"][0:16, 0:128], start=True, stop=True),
                             reads=["fc_sb", "rstd16b"], writes=["pb5"])
                        osl = b[f"hid{8 + fc}"][:, 512 + ci * 128:512 + ci * 128 + 128]
                        p.op("vector", lambda e, osl=osl, fc=fc: e.tensor_tensor(
                            b["og_f"][:, 0:128], osl, b["pb5"][:, 0:128], op=ALU.mult),
                            reads=[f"hid{8 + fc}", "pb5"], writes=["og_f"])
                        p.op("vector", lambda e, osl=osl, fc=fc: e.tensor_scalar(
                            osl, b["og_f"][:, 0:128], b["nw_col"][:, fc:fc + 1], None, op0=ALU.mult),
                            reads=["og_f", "nw_col"], writes=[f"hid{8 + fc}"])
            for fc in range(8):
                p.op("vector", lambda e, fc=fc: e.tensor_copy(
                    b["Lcarry"][:, fc:fc + 1], b["Lfm"][:, fc * GLA_SUP + GLA_SUP - 1:fc * GLA_SUP + GLA_SUP]),
                    reads=["Lfm"], writes=["Lcarry"])
            if passB:
                for fo in range(8):
                    sl = fo % 2
                    wreg = f"w1s{sl}"
                    p.dma("gpsimd", b[wreg][:, :], wo_ap[fo], writes=[wreg], tag=f"glao{fo}")
                    pa = f"pb{fo % 2}"
                    def mm_wo(e, wreg=wreg, pa=pa):
                        r = None
                        for fi in range(8):
                            r = e.matmul(b[pa][:, 0:GLA_SUP], b[wreg][:, fi * 128:(fi + 1) * 128],
                                         b[f"hid{8 + fi}"][:, 512:512 + GLA_SUP], start=fi == 0, stop=fi == 7)
                        return r
                    p.mm(mm_wo, reads=[wreg] + [f"hid{8 + fi}" for fi in range(8)], writes=[pa])
                    p.op("vector", lambda e, pa=pa, fo=fo, t0=t0: e.tensor_tensor(
                        resid[fo][:, t0:t0 + GLA_SUP], resid[fo][:, t0:t0 + GLA_SUP], b[pa][:, 0:GLA_SUP],
                        op=ALU.add),
                        reads=[pa, resid_r + str(fo)], writes=[resid_r + str(fo)])
        if not passB:
            p.op("scalar", lambda e: e.activation(b["zsum"][:, 0:8], b["Lcarry"][:, 0:8], AF.Exp),
                 reads=["Lcarry"], writes=["zsum"])
            p.op("vector", lambda e: e.tensor_copy(b["gsum_st"][:, 0:512], b["S_all"][:, :]),
                 reads=["S_all"], writes=["gsum_st"])
            p.op("vector", lambda e: e.tensor_copy(b["gsum_st"][:, 512:520], b["zsum"][:, 0:8]),
                 reads=["zsum"], writes=["gsum_st"])
            p.dma("sync", gsum_out_ap[bh], b["gsum_st"][:, :], reads=["gsum_st"],
                  writes=["gsum_out"], stream="gsum_out")


def emit_gla_fold(ctx, summ_all_ap, core_id):
    p = ctx.p
    b = ctx.b
    p.op("vector", lambda e: e.memset(b["S_init"][:, :], 0.0), writes=["S_init"])
    for bh in range(2):
        for j in range(core_id):
            p.dma("sync", b["gsum_st"][:, :], summ_all_ap[j][bh], writes=["gsum_st"],
                  stream="gsum_in")
            for fc in range(8):
                p.op("vector", lambda e, bh=bh, fc=fc: e.scalar_tensor_tensor(
                    b["S_init"][:, bh * 512 + fc * 64:bh * 512 + (fc + 1) * 64],
                    b["S_init"][:, bh * 512 + fc * 64:bh * 512 + (fc + 1) * 64],
                    b["gsum_st"][:, 512 + fc:512 + fc + 1],
                    b["gsum_st"][:, fc * 64:(fc + 1) * 64], op0=ALU.mult, op1=ALU.add),
                    reads=["S_init", "gsum_st"], writes=["S_init"])


def prelay_w64(W):
    return np.ascontiguousarray(W.reshape(8, 128, 64).transpose(1, 0, 2).reshape(128, 512))



def alloc_l1ffn_scratch(ctx):
    sb = ctx.sb
    sb("skv", [60, 128], F32)
    sb("kap60", [60, 128], F32)
    sb("selkv", [60, 128], F32)
    sb("selq", [4, 128], F32)
    sb("skq", [4, 128], F32)
    sb("rt_t", [60, 128], F32)
    sb("rt_e", [60, 128], F32)
    sb("rt_cols", [60, 8], F32)
    sb("rt_row", [1, 64], F32)
    sb("rank_f", [60, 128], F32)
    sb("rankT", [128, 60], F32)
    sb("rankTi", [128, 60], mybir.dt.int32)
    sb("sel_f", [60, 128], F32)
    sb("qsel_save", [4, 128 * 4], BF16)
    sb("iota128", [60, 128], F32)
    sb("lt60", [60, 60], F32)
    sb("ones60", [60, 1], F32)
    sb("onesr128", [1, 128], F32)
    sb("qpos1", [128, 4], F32)
    sb("rms_c", [128, 8], F32)
    
    sb("pm_bf", [128, 256], BF16)
    sb("vtm_r", [128, 1024], BF16)
    sb("nullk_sb", [128, 32], BF16)
    sb("cnbuf", [128, 1024], BF16)
    sb("ropeC", [128, 1536], BF16)
    sb("ropeS", [128, 1536], BF16)
    sb("rmat", [128, 128], BF16)
    sb("kmask", [128, 8 * 512], BF16)
    sb("lng_c", [128, 8], F32)
    sb("lnb_c", [128, 8], F32)
    sb("xtile", [128, 1024], BF16)
    
    sb("lq", [128, 512], BF16)
    sb("lkv", [128, 2048], BF16)
    
    sb("skq_bf", [4, 128], BF16)
    sb("rsel4", [4, 512], BF16)
    sb("cwrk", [128, 1024], BF16)
    sb("posi32", [128, 2], mybir.dt.int32)
    sb("onesbf1", [1, 1], BF16)
    
    sb("nullv_s", [1, 64], BF16)
    ctx.p.op("vector", lambda e: e.memset(ctx.b["onesbf1"][:], 1.0), writes=["onesbf1"])


def emit_router_pack(ctx, s_row_ap, maskadd_ap, npart, width, logk, sel_out, sel_r,
                     kappa_out, kappa_r, uid):
    p = ctx.p
    b = ctx.b
    S = b["skq"] if npart == 4 else b["skv"]
    Sr = "skq" if npart == 4 else "skv"
    t, ee = b["rt_t"], b["rt_e"]
    cols, row = b["rt_cols"], b["rt_row"]
    p.dma("sync", S[0:npart, 0:width], s_row_ap.rearrange("(p f) -> p f", f=width),
          writes=[Sr], stream="rtld", tag=f"rt{uid}")
    if maskadd_ap is not None:
        p.dma("sync", t[0:npart, 0:width], maskadd_ap, writes=["rt_t"], stream="rtld2")
        p.op("vector", lambda e: e.tensor_tensor(S[0:npart, 0:width], S[0:npart, 0:width],
                                                 t[0:npart, 0:width], op=ALU.add),
             reads=[Sr, "rt_t"], writes=[Sr])
        p.op("vector", lambda e: e.tensor_copy(t[0:npart, 0:width], S[0:npart, 0:width]),
             reads=[Sr], writes=["rt_t"])
        p.op("vector", lambda e: e.tensor_scalar_min(t[0:npart, 0:width], t[0:npart, 0:width], 0.0),
             reads=["rt_t"], writes=["rt_t"])
        p.dma("sync", t[0:npart, 0:width], maskadd_ap, writes=["rt_t"], stream="rtld2")
    else:
        p.op("vector", lambda e: e.memset(t[0:npart, 0:width], 0.0), writes=["rt_t"])

    eps_sched = [max(4.0 * 0.7 ** i, 0.03) for i in range(20)]
    for it, ce in enumerate(eps_sched):
        p.op("vector", lambda e: e.reduce_max(cols[0:npart, 0:1], t[0:npart, 0:width],
                                              axis=mybir.AxisListType.X),
             reads=["rt_t"], writes=["rt_cols"])
        p.mm(lambda e: e.transpose(ctx.b["pb3"][0:1, 0:npart], cols[0:npart, 0:1], b["ident_f32"][0:npart, 0:npart]),
             reads=["rt_cols", "ident_f32"], writes=["pb3"])
        p.op("vector", lambda e: e.tensor_copy(row[0:1, 0:npart], ctx.b["pb3"][0:1, 0:npart]),
             reads=["pb3"], writes=["rt_row"])
        p.op("vector", lambda e: e.reduce_max(row[0:1, 32:33], row[0:1, 0:npart],
                                              axis=mybir.AxisListType.X),
             reads=["rt_row"], writes=["rt_row"])
        p.op("vector", lambda e, ce=ce: e.tensor_scalar_mul(row[0:1, 33:34], row[0:1, 32:33], -1.0 / ce),
             reads=["rt_row"], writes=["rt_row"])
        p.mm(lambda e: e.matmul(ctx.b["pb3"][0:npart, 0:1], b["onesr128"][0:1, 0:npart], row[0:1, 33:34],
                                start=True, stop=True),
             reads=["rt_row", "onesr128"], writes=["pb3"])
        p.op("vector", lambda e: e.tensor_copy(cols[0:npart, 1:2], ctx.b["pb3"][0:npart, 0:1]),
             reads=["pb3"], writes=["rt_cols"])
        p.op("scalar", lambda e, ce=ce: e.activation(ee[0:npart, 0:width], t[0:npart, 0:width], AF.Exp,
                                                     scale=1.0 / ce, bias=cols[0:npart, 1:2],
                                                     accum_out=cols[0:npart, 2:3]),
             reads=["rt_t", "rt_cols"], writes=["rt_e", "rt_cols"])
        p.mm(lambda e: e.matmul(ctx.b["pb3"][0:1, 0:1], b["ones60"][0:npart, 0:1], cols[0:npart, 2:3],
                                start=True, stop=True),
             reads=["rt_cols", "ones60"], writes=["pb3"])
        p.op("scalar", lambda e, ce=ce: e.activation(row[0:1, 34:35], ctx.b["pb3"][0:1, 0:1], AF.Ln),
             reads=["pb3"], writes=["rt_row"])
        p.op("vector", lambda e, ce=ce: e.tensor_scalar(row[0:1, 35:36], row[0:1, 34:35], -ce, None, op0=ALU.mult),
             reads=["rt_row"], writes=["rt_row"])
        p.op("vector", lambda e, ce=ce, logk=logk: e.tensor_tensor(
            row[0:1, 35:36], row[0:1, 35:36], row[0:1, 32:33], op=ALU.subtract),
            reads=["rt_row"], writes=["rt_row"])
        p.op("vector", lambda e, ce=ce, logk=logk: e.tensor_scalar_add(
            row[0:1, 36:37], row[0:1, 35:36], ce * logk),
            reads=["rt_row"], writes=["rt_row"])
        p.op("vector", lambda e: e.tensor_scalar_mul(row[0:1, 37:38], row[0:1, 36:37], -1.0),
             reads=["rt_row"], writes=["rt_row"])
        p.mm(lambda e: e.matmul(ctx.b["pb3"][0:npart, 0:1], b["onesr128"][0:1, 0:npart], row[0:1, 37:38],
                                start=True, stop=True),
             reads=["rt_row", "onesr128"], writes=["pb3"])
        p.op("vector", lambda e: e.tensor_copy(cols[0:npart, 3:4], ctx.b["pb3"][0:npart, 0:1]),
             reads=["pb3"], writes=["rt_cols"])
        p.op("vector", lambda e: e.tensor_scalar(t[0:npart, 0:width], S[0:npart, 0:width],
                                                 cols[0:npart, 3:4], None, op0=ALU.min),
             reads=[Sr, "rt_cols"], writes=["rt_t"])
    p.op("vector", lambda e: e.tensor_scalar(b[kappa_out][0:npart, 0:width], t[0:npart, 0:width],
                                             cols[0:npart, 3:4], None, op0=ALU.subtract),
         reads=["rt_t", "rt_cols"], writes=[kappa_r])


def emit_topk_sel(ctx, npart, width, K_sel, kappa, kappa_r, sel_out, sel_r,
                  static_firstN=None, uid=""):
    p = ctx.p
    b = ctx.b
    row = b["rt_row"]
    cols = b["rt_cols"]
    ka = b[kappa]

    if static_firstN is not None:
        p.op("vector", lambda e: e.tensor_scalar(b[sel_out][0:npart, 0:width],
                                                 b["iota128"][0:npart, 0:width],
                                                 float(static_firstN), None, op0=ALU.is_lt),
             reads=["iota128"], writes=[sel_r])
        return


    def sel_update(dst_slot, gate_slot, new_slot, take_if_gate):
        p.op("vector", lambda e: e.tensor_tensor(row[0:1, 52:53], row[0:1, new_slot:new_slot + 1],
                                                 row[0:1, dst_slot:dst_slot + 1], op=ALU.subtract),
             reads=["rt_row"], writes=["rt_row"])
        if take_if_gate:
            p.op("vector", lambda e: e.tensor_tensor(row[0:1, 52:53], row[0:1, 52:53],
                                                     row[0:1, gate_slot:gate_slot + 1], op=ALU.mult),
                 reads=["rt_row"], writes=["rt_row"])
        else:
            p.op("vector", lambda e: e.tensor_scalar(row[0:1, 53:54], row[0:1, gate_slot:gate_slot + 1],
                                                     -1.0, 1.0, op0=ALU.mult, op1=ALU.add),
                 reads=["rt_row"], writes=["rt_row"])
            p.op("vector", lambda e: e.tensor_tensor(row[0:1, 52:53], row[0:1, 52:53],
                                                     row[0:1, 53:54], op=ALU.mult),
                 reads=["rt_row"], writes=["rt_row"])
        p.op("vector", lambda e: e.tensor_tensor(row[0:1, dst_slot:dst_slot + 1],
                                                 row[0:1, dst_slot:dst_slot + 1],
                                                 row[0:1, 52:53], op=ALU.add),
             reads=["rt_row"], writes=["rt_row"])
    def count_gt(src_reg, src_ap, thr_slot, out_slot):
        p.mm(lambda e: e.matmul(ctx.b["pb3"][0:npart, 0:1], b["onesr128"][0:1, 0:npart],
                                row[0:1, thr_slot:thr_slot + 1], start=True, stop=True),
             reads=["rt_row", "onesr128"], writes=["pb3"])
        p.op("vector", lambda e: e.tensor_copy(cols[0:npart, 4:5], ctx.b["pb3"][0:npart, 0:1]),
             reads=["pb3"], writes=["rt_cols"])
        p.op("vector", lambda e: e.tensor_scalar(b["rt_e"][0:npart, 0:width], src_ap,
                                                 cols[0:npart, 4:5], 0.0, op0=ALU.is_gt,
                                                 op1=ALU.add, accum_out=cols[0:npart, 5:6]),
             reads=[src_reg, "rt_cols"], writes=["rt_e", "rt_cols"])
        p.mm(lambda e: e.matmul(ctx.b["pb3"][0:1, 0:1], b["ones60"][0:npart, 0:1], cols[0:npart, 5:6],
                                start=True, stop=True),
             reads=["rt_cols", "ones60"], writes=["pb3"])
        p.op("vector", lambda e: e.tensor_copy(row[0:1, out_slot:out_slot + 1], ctx.b["pb3"][0:1, 0:1]),
             reads=["pb3"], writes=["rt_row"])

    p.op("scalar", lambda e: e.activation(b["rt_e"][0:npart, 0:width], ka[0:npart, 0:width],
                                          AF.Exp, scale=1.0 / 0.03),
         reads=[kappa_r], writes=["rt_e"])
    p.op("vector", lambda e: e.tensor_scalar(b["rt_e"][0:npart, 0:width], b["rt_e"][0:npart, 0:width],
                                             0.0, None, op0=ALU.is_equal),
         reads=["rt_e"], writes=["rt_e"])
    p.op("vector", lambda e: e.tensor_scalar(b["sel_f"][0:npart, 0:width], ka[0:npart, 0:width],
                                             4200.0, None, op0=ALU.add),
         reads=[kappa_r], writes=["sel_f"])
    p.op("vector", lambda e: e.tensor_tensor(b["sel_f"][0:npart, 0:width], b["sel_f"][0:npart, 0:width],
                                             b["rt_e"][0:npart, 0:width], op=ALU.mult),
         reads=["sel_f", "rt_e"], writes=["sel_f"])
    p.op("vector", lambda e: e.tensor_tensor(b[kappa][0:npart, 0:width], ka[0:npart, 0:width],
                                             b["sel_f"][0:npart, 0:width], op=ALU.subtract),
         reads=[kappa_r, "sel_f"], writes=[kappa_r])

    p.op("vector", lambda e: e.memset(row[0:1, 40:41], -4300.0), writes=["rt_row"])
    p.op("vector", lambda e: e.memset(row[0:1, 41:42], 1e-4), writes=["rt_row"])
    for it in range(26):
        p.op("vector", lambda e: e.tensor_tensor(row[0:1, 42:43], row[0:1, 40:41], row[0:1, 41:42], op=ALU.add),
             reads=["rt_row"], writes=["rt_row"])
        p.op("vector", lambda e: e.tensor_scalar_mul(row[0:1, 42:43], row[0:1, 42:43], 0.5),
             reads=["rt_row"], writes=["rt_row"])
        count_gt(kappa_r, ka[0:npart, 0:width], 42, 43)
        p.op("vector", lambda e: e.tensor_scalar(row[0:1, 44:45], row[0:1, 43:44], float(K_sel), None, op0=ALU.is_ge),
             reads=["rt_row"], writes=["rt_row"])
        sel_update(40, 44, 42, True)
        sel_update(41, 44, 42, False)
    p.op("vector", lambda e: e.tensor_copy(row[0:1, 42:43], row[0:1, 41:42]), reads=["rt_row"], writes=["rt_row"])
    count_gt(kappa_r, ka[0:npart, 0:width], 42, 46)
    p.op("vector", lambda e: e.tensor_copy(b[sel_out][0:npart, 0:width], b["rt_e"][0:npart, 0:width]),
         reads=["rt_e"], writes=[sel_r])
    p.op("vector", lambda e: e.tensor_copy(row[0:1, 42:43], row[0:1, 40:41]), reads=["rt_row"], writes=["rt_row"])
    count_gt(kappa_r, ka[0:npart, 0:width], 42, 45)
    p.op("vector", lambda e: e.tensor_tensor(b["sel_f"][0:npart, 0:width], b["rt_e"][0:npart, 0:width],
                                             b[sel_out][0:npart, 0:width], op=ALU.subtract),
         reads=["rt_e", sel_r], writes=["sel_f"])
    p.op("vector", lambda e: e.tensor_scalar(row[0:1, 47:48], row[0:1, 46:47], -1.0, float(K_sel),
                                             op0=ALU.mult, op1=ALU.add),
         reads=["rt_row"], writes=["rt_row"])
    p.op("vector", lambda e: e.memset(row[0:1, 48:49], 0.0), writes=["rt_row"])
    p.op("vector", lambda e: e.memset(row[0:1, 49:50], float(npart * width + 1)), writes=["rt_row"])
    p.op("vector", lambda e: e.tensor_scalar_add(b["rt_e"][0:npart, 0:width], b["iota128"][0:npart, 0:width], 1.0),
         reads=["iota128"], writes=["rt_e"])
    p.op("vector", lambda e: e.tensor_tensor(b["sel_f"][0:npart, 0:width], b["sel_f"][0:npart, 0:width],
                                             b["rt_e"][0:npart, 0:width], op=ALU.mult),
         reads=["sel_f", "rt_e"], writes=["sel_f"])
    for it in range(14):
        p.op("vector", lambda e: e.tensor_tensor(row[0:1, 42:43], row[0:1, 48:49], row[0:1, 49:50], op=ALU.add),
             reads=["rt_row"], writes=["rt_row"])
        p.op("vector", lambda e: e.tensor_scalar_mul(row[0:1, 42:43], row[0:1, 42:43], 0.5),
             reads=["rt_row"], writes=["rt_row"])
        count_gt("sel_f", b["sel_f"][0:npart, 0:width], 42, 43)
        p.op("vector", lambda e: e.tensor_scalar(row[0:1, 45:46], row[0:1, 43:44], -1.0, None, op0=ALU.mult),
             reads=["rt_row"], writes=["rt_row"])
        p.op("vector", lambda e: e.memset(row[0:1, 42:43], 0.5), writes=["rt_row"])
        count_gt("sel_f", b["sel_f"][0:npart, 0:width], 42, 43)
        p.op("vector", lambda e: e.tensor_tensor(row[0:1, 43:44], row[0:1, 43:44], row[0:1, 45:46], op=ALU.add),
             reads=["rt_row"], writes=["rt_row"])
        p.op("vector", lambda e: e.tensor_tensor(row[0:1, 50:51], row[0:1, 48:49], row[0:1, 49:50], op=ALU.add),
             reads=["rt_row"], writes=["rt_row"])
        p.op("vector", lambda e: e.tensor_scalar_mul(row[0:1, 50:51], row[0:1, 50:51], 0.5),
             reads=["rt_row"], writes=["rt_row"])
        p.op("vector", lambda e: e.tensor_scalar(row[0:1, 44:45], row[0:1, 43:44], None, None, op0=ALU.bypass)
             if False else e.tensor_tensor(row[0:1, 44:45], row[0:1, 43:44], row[0:1, 47:48], op=ALU.is_ge),
             reads=["rt_row"], writes=["rt_row"])
        sel_update(49, 44, 50, True)
        sel_update(48, 44, 50, False)
    p.mm(lambda e: e.matmul(ctx.b["pb3"][0:npart, 0:1], b["onesr128"][0:1, 0:npart], row[0:1, 49:50],
                            start=True, stop=True),
         reads=["rt_row", "onesr128"], writes=["pb3"])
    p.op("vector", lambda e: e.tensor_copy(cols[0:npart, 4:5], ctx.b["pb3"][0:npart, 0:1]),
         reads=["pb3"], writes=["rt_cols"])
    p.op("vector", lambda e: e.tensor_scalar(b["rt_e"][0:npart, 0:width], b["sel_f"][0:npart, 0:width],
                                             cols[0:npart, 4:5], 0.5, op0=ALU.is_le, op1=ALU.subtract),
         reads=["sel_f", "rt_cols"], writes=["rt_e"])
    p.op("vector", lambda e: e.tensor_scalar(b["sel_f"][0:npart, 0:width], b["sel_f"][0:npart, 0:width],
                                             0.5, None, op0=ALU.is_gt),
         reads=["sel_f"], writes=["sel_f"])
    p.op("vector", lambda e: e.tensor_scalar(b["rt_e"][0:npart, 0:width], b["rt_e"][0:npart, 0:width],
                                             0.0, None, op0=ALU.is_gt),
         reads=["rt_e"], writes=["rt_e"])
    p.op("vector", lambda e: e.tensor_tensor(b["sel_f"][0:npart, 0:width], b["sel_f"][0:npart, 0:width],
                                             b["rt_e"][0:npart, 0:width], op=ALU.mult),
         reads=["sel_f", "rt_e"], writes=["sel_f"])
    p.op("vector", lambda e: e.tensor_tensor(b[sel_out][0:npart, 0:width], b[sel_out][0:npart, 0:width],
                                             b["sel_f"][0:npart, 0:width], op=ALU.add),
         reads=[sel_r, "sel_f"], writes=[sel_r])


def emit_rank_scatter(ctx, npart, sel_name, xg_row0_ap_fn, dst_buf_ap, dst_r,
                      n_blocks, uid, pos_dst_ap=None, pos_r=None, n_slots=1024):
    p = ctx.p
    b = ctx.b
    sel = b[sel_name]
    W = 128
    p.op("vector", lambda e: e.reduce_sum(b["rt_cols"][0:npart, 6:7], sel[0:npart, 0:W],
                                          axis=mybir.AxisListType.X),
         reads=[sel_name], writes=["rt_cols"])
    p.mm(lambda e: e.matmul(ctx.b["pb3"][0:npart, 0:1], b["lt60"][0:npart, 0:npart],
                            b["rt_cols"][0:npart, 6:7], start=True, stop=True),
         reads=["rt_cols", "lt60"], writes=["pb3"])
    p.op("vector", lambda e: e.tensor_copy(b["rt_cols"][0:npart, 7:8], ctx.b["pb3"][0:npart, 0:1]),
         reads=["pb3"], writes=["rt_cols"])
    p.op("vector", lambda e: e.tensor_tensor_scan(b["rank_f"][0:npart, 0:W], sel[0:npart, 0:W],
                                                  sel[0:npart, 0:W],
                                                  b["rt_cols"][0:npart, 7:8], ALU.add, ALU.bypass),
         reads=[sel_name, "rt_cols"], writes=["rank_f"])
    p.op("vector", lambda e: e.tensor_scalar(b["rt_e"][0:npart, 0:W], sel[0:npart, 0:W],
                                             -200000.0, 200000.0, op0=ALU.mult, op1=ALU.add),
         reads=[sel_name], writes=["rt_e"])
    p.op("vector", lambda e: e.tensor_scalar_add(b["rank_f"][0:npart, 0:W], b["rank_f"][0:npart, 0:W], -1.0),
         reads=["rank_f"], writes=["rank_f"])
    p.op("vector", lambda e: e.tensor_tensor(b["rank_f"][0:npart, 0:W], b["rank_f"][0:npart, 0:W],
                                             b["rt_e"][0:npart, 0:W], op=ALU.add),
         reads=["rank_f", "rt_e"], writes=["rank_f"])
    p.op("vector", lambda e: e.tensor_scalar_min(b["rank_f"][0:npart, 0:W], b["rank_f"][0:npart, 0:W],
                                                 float(n_slots)),
         reads=["rank_f"], writes=["rank_f"])
    p.mm(lambda e: e.transpose(ctx.b["pb3"][0:W, 0:npart], b["rank_f"][0:npart, 0:W],
                               b["ident_f32"][0:npart, 0:npart]),
         reads=["rank_f", "ident_f32"], writes=["pb3"])
    p.op("vector", lambda e: e.tensor_copy(b["rankT"][0:W, 0:npart], ctx.b["pb3"][0:W, 0:npart]),
         reads=["pb3"], writes=["rankT"])
    p.op("vector", lambda e: e.tensor_copy(b["rankTi"][0:W, 0:npart], b["rankT"][0:W, 0:npart]),
         reads=["rankT"], writes=["rankTi"])
    from concourse.bass import IndirectOffsetOnAxis
    for c in range(n_blocks):
        p.dma("sync", b["xtile"][:, :], xg_row0_ap_fn(c), reads=["xg"], writes=["xtile"],
              stream="xtile", tag=f"rl{uid}{c}")
        p.op("gpsimd", lambda e, c=c: e.indirect_dma_start(
            dst_buf_ap, IndirectOffsetOnAxis(b["rankTi"][0:W, c:c + 1], 0),
            b["xtile"][:, :], None, oob_is_err=False),
            reads=["rankTi", "xtile"], writes=[dst_r], dma=True, stream=dst_r, tag=f"rs{uid}{c}")
        if pos_dst_ap is not None:
            p.op("gpsimd", lambda e, c=c: e.indirect_dma_start(
                pos_dst_ap, IndirectOffsetOnAxis(b["rankTi"][0:W, c:c + 1], 0),
                b["qpos1"][0:W, c:c + 1], None, oob_is_err=False),
                reads=["rankTi", "qpos1"], writes=[pos_r], dma=True, stream=pos_r, tag=f"rsp{uid}{c}")


def emit_rms_tm_load(ctx, src_ap, n_rows, normbc_name, g, dst_fm, dst_r_list, dst_col0, uid, src_r=None):
    p = ctx.p
    b = ctx.b
    nap = ctx.qnb_ap if normbc_name == "qnb" else ctx.cnb_ap
    p.dma("sync", b["cnbuf"][:, :], nap[:, g * 1024:(g + 1) * 1024], writes=["cnbuf"],
          stream="cnbuf", tag=f"nb{uid}")
    for tile in range(cdiv(n_rows, 128)):
        r0 = tile * 128
        nr = min(128, n_rows - r0)
        p.dma("sync", b["xtile"][0:nr, :], src_ap[r0:r0 + nr, :],
              reads=[src_r] if src_r else [], writes=["xtile"],
              stream="xtile", tag=f"rms{uid}{tile}")
        p.op("scalar", lambda e, nr=nr: e.activation(b["cwrk"][0:nr, :], b["xtile"][0:nr, :], AF.Square,
                                                     accum_out=b["rms_c"][0:nr, 0:1]),
             reads=["xtile"], writes=["cwrk", "rms_c"])
        p.op("scalar", lambda e, nr=nr: e.activation(b["rms_c"][0:nr, 1:2], b["rms_c"][0:nr, 0:1], AF.Sqrt),
             reads=["rms_c"], writes=["rms_c"])
        p.op("vector", lambda e, nr=nr: e.tensor_scalar_max(b["rms_c"][0:nr, 1:2], b["rms_c"][0:nr, 1:2], 1e-12),
             reads=["rms_c"], writes=["rms_c"])
        p.op("vector", lambda e, nr=nr: e.reciprocal(b["rms_c"][0:nr, 2:3], b["rms_c"][0:nr, 1:2]),
             reads=["rms_c"], writes=["rms_c"])
        p.op("vector", lambda e, nr=nr: e.tensor_scalar_mul(b["rms_c"][0:nr, 2:3], b["rms_c"][0:nr, 2:3], 32.0),
             reads=["rms_c"], writes=["rms_c"])
        p.op("vector", lambda e, nr=nr: e.tensor_scalar(b["xtile"][0:nr, :], b["xtile"][0:nr, :],
                                                        b["rms_c"][0:nr, 2:3], None, op0=ALU.mult),
             reads=["xtile", "rms_c"], writes=["xtile"])
        p.op("vector", lambda e, nr=nr: e.tensor_tensor(
            b["xtile"][0:nr, :], b["xtile"][0:nr, :], b["cnbuf"][0:nr, :],
            op=ALU.mult),
            reads=["xtile", "cnbuf"], writes=["xtile"])
        for fc in range(8):
            p.mm(lambda e, fc=fc: e.transpose(b["pbt"][:, 0:128], b["xtile"][:, fc * 128:(fc + 1) * 128],
                                              b["ident_bf"][:, :]),
                 reads=["xtile", "ident_bf"], writes=["pbt"])
            p.op("vector", lambda e, fc=fc, r0=r0, nr=nr: e.tensor_copy(
                dst_fm[fc][:, dst_col0 + r0:dst_col0 + r0 + nr], b["pbt"][:, 0:nr]),
                reads=["pbt"], writes=[dst_r_list[fc]])


def emit_grouped_attn(ctx, kvbuf_ap, qbuf_ap, n_kv_tiles, g, wq_ap, wkv_ap, wout_ap,
                      acc_ap, acc_r, posidx_ap, uid, pos_r="postab", kv_r="kvbuf", q_r="qbuf"):
    p = ctx.p
    b = ctx.b
    NQ = 256
    xq_fm = [b[f"hid{c}"] for c in range(8)]
    xc_fm = [b[f"hid{8 + c}"] for c in range(8)]
    k_fm = [b[f"hid{16 + c}"] for c in range(6)] + [b["va0"], b["va1"]]
    k_fm_r = [f"hid{16 + c}" for c in range(6)] + ["va0", "va1"]
    emit_rms_tm_load(ctx, qbuf_ap, NQ, "qnb", g, xq_fm, [f"hid{c}" for c in range(8)], 0, uid + "q", src_r=q_r)
    emit_rms_tm_load(ctx, kvbuf_ap, 1024, "cnb", g, xc_fm, [f"hid{8 + c}" for c in range(8)], 0, uid + "c", src_r=kv_r)
    xr_q = [f"hid{c}" for c in range(8)]
    xr_c = [f"hid{8 + c}" for c in range(8)]
    for fo in range(8):
        sl = fo % 2
        wreg = f"w1s{sl}"
        p.dma("gpsimd", b[wreg][:, :], wq_ap[fo], writes=[wreg], tag=f"gq{uid}{fo}")
        def mm_q(e, wreg=wreg):
            r = None
            for fi in range(8):
                r = e.matmul(b["pb0"][:, 0:NQ], b[wreg][:, fi * 128:(fi + 1) * 128],
                             xq_fm[fi][:, 0:NQ], start=fi == 0, stop=fi == 7)
            return r
        p.mm(mm_q, reads=[wreg] + xr_q, writes=["pb0"])
        p.op("scalar", lambda e, fo=fo: e.activation(xq_fm[fo][:, 256:256 + NQ], b["pb0"][:, 0:NQ], AF.Copy),
             reads=["pb0"], writes=[f"hid{fo}"])
    for fo in range(8):
        sl = fo % 2
        wreg = f"w1s{sl}"
        p.dma("gpsimd", b[wreg][:, :], wkv_ap[fo], writes=[wreg], tag=f"gk{uid}{fo}")
        for nt in range(2):
            c0 = nt * 512
            cw = 512
            pa = f"pb{nt % 2}"
            def mm_k(e, wreg=wreg, c0=c0, cw=cw, pa=pa):
                r = None
                for fi in range(8):
                    r = e.matmul(b[pa][:, 0:cw], b[wreg][:, fi * 128:(fi + 1) * 128],
                                 xc_fm[fi][:, c0:c0 + cw], start=fi == 0, stop=fi == 7)
                return r
            p.mm(mm_k, reads=[wreg] + xr_c, writes=[pa])
            p.op("scalar", lambda e, fo=fo, c0=c0, cw=cw, pa=pa: e.activation(
                k_fm[fo][:, c0:c0 + cw], b[pa][:, 0:cw], AF.Copy),
                reads=[pa], writes=[k_fm_r[fo]])
    for fo in range(8):
        sl = fo % 2
        wreg = f"w1s{sl}"
        p.dma("gpsimd", b[wreg][:, :], wkv_ap[8 + fo], writes=[wreg], tag=f"gv{uid}{fo}")
        for nt in range(2):
            c0 = nt * 512
            cw = 512
            pa = f"pb{nt % 2}"
            def mm_v(e, wreg=wreg, c0=c0, cw=cw, pa=pa):
                r = None
                for fi in range(8):
                    r = e.matmul(b[pa][:, 0:cw], b[wreg][:, fi * 128:(fi + 1) * 128],
                                 xc_fm[fi][:, c0:c0 + cw], start=fi == 0, stop=fi == 7)
                return r
            p.mm(mm_v, reads=[wreg] + xr_c, writes=[pa])
            p.op("scalar", lambda e, c0=c0, cw=cw, pa=pa: e.activation(
                b["xtile"][:, c0:c0 + cw], b[pa][:, 0:cw], AF.Copy),
                reads=[pa], writes=["xtile"])
        for kt in range(n_kv_tiles):
            p.mm(lambda e, kt=kt: e.transpose(b["pbt"][:, 0:128], b["xtile"][:, kt * 128:(kt + 1) * 128],
                                              b["ident_bf"][:, :]),
                 reads=["xtile", "ident_bf"], writes=["pbt"])
            p.op("vector", lambda e, kt=kt: e.tensor_copy(b["vtm_r"][:, kt * 128:(kt + 1) * 128],
                                                          b["pbt"][:, 0:128]),
                 reads=["pbt"], writes=["vtm_r"])
        for hh in range(2):
            h = 2 * fo + hh
            hs = hh * 64
            for kt in range(n_kv_tiles):
                p.mm(lambda e, kt=kt, hs=hs, fo=fo: e.matmul(
                    b["pb0"][:, 0:NQ], k_fm[fo][hs:hs + 64, kt * 128:(kt + 1) * 128],
                    xq_fm[fo][hs:hs + 64, 256:256 + NQ], start=True, stop=True),
                    reads=[k_fm_r[fo], f"hid{fo}"], writes=["pb0"])
                p.op("scalar", lambda e: e.activation(b["pm_bf"][:, 0:NQ], b["pb0"][:, 0:NQ],
                                                      AF.Exp, scale=0.125),
                     reads=["pb0"], writes=["pm_bf"])
                p.mm(lambda e, kt=kt, first=kt == 0: e.matmul(
                    b["pb4"][0:1, 0:NQ], b["ones_col_bf"][:, 0:1], b["pm_bf"][:, 0:NQ],
                    start=first, stop=False),
                    reads=["pm_bf", "ones_col_bf"], writes=["pb4"])
                p.mm(lambda e, kt=kt, hs=hs, first=kt == 0: e.matmul(
                    b["pb2"][0:64, 0:NQ], b["vtm_r"][:, kt * 128 + hs:kt * 128 + hs + 64],
                    b["pm_bf"][:, 0:NQ], start=first, stop=False),
                    reads=["vtm_r", "pm_bf"], writes=["pb2"])
            p.mm(lambda e, h=h, hs=hs, fo=fo: e.matmul(b["pb1"][0:1, 0:NQ], b["nullk_sb"][hs:hs + 64, g * 16 + h:g * 16 + h + 1],
                                         xq_fm[fo][hs:hs + 64, 256:256 + NQ], start=True, stop=True),
                 reads=["nullk_sb", f"hid{fo}"], writes=["pb1"])
            p.op("scalar", lambda e: e.activation(b["pm_bf"][0:1, 0:NQ], b["pb1"][0:1, 0:NQ],
                                                  AF.Exp, scale=0.125),
                 reads=["pb1"], writes=["pm_bf"])
            p.mm(lambda e: e.matmul(b["pb4"][0:1, 0:NQ], b["onesbf1"][0:1, 0:1], b["pm_bf"][0:1, 0:NQ],
                                    start=False, stop=True),
                 reads=["pm_bf", "onesbf1"], writes=["pb4"])
            p.dma("sync", b["nullv_s"][0:1, 0:64], ctx.nullv_ap[0:1, (g * 16 + h) * 64:(g * 16 + h) * 64 + 64],
                  writes=["nullv_s"], stream="nullv_s", tag=f"nv{uid}{h}")
            p.mm(lambda e, h=h: e.matmul(b["pb2"][0:64, 0:NQ],
                                         b["nullv_s"][0:1, 0:64],
                                         b["pm_bf"][0:1, 0:NQ], start=False, stop=True),
                 reads=["nullv_s", "pm_bf"], writes=["pb2"])
            p.op("vector", lambda e, h=h: e.tensor_copy(b["ln_rows"][0:1, 0:NQ],
                                                        b["pb4"][0:1, 0:NQ]),
                 reads=["pb4"], writes=["ln_rows"])
            p.op("vector", lambda e, h=h: e.reciprocal(b["ln_rows"][0:1, 0:NQ],
                                                       b["ln_rows"][0:1, 0:NQ]),
                 reads=["ln_rows"], writes=["ln_rows"])
            p.mm(lambda e, h=h: e.matmul(b["pb5"][0:64, 0:NQ], b["onesr128"][0:1, 0:64],
                                         b["ln_rows"][0:1, 0:NQ], start=True, stop=True),
                 reads=["ln_rows", "onesr128"], writes=["pb5"])
            p.op("vector", lambda e: e.tensor_copy(b["ln_bc"][0:64, 0:NQ], b["pb5"][0:64, 0:NQ]),
                 reads=["pb5"], writes=["ln_bc"])
            p.op("vector", lambda e, hs=hs, fo=fo: e.tensor_tensor(
                xq_fm[fo][hs:hs + 64, 512:512 + NQ], b["pb2"][0:64, 0:NQ], b["ln_bc"][0:64, 0:NQ],
                op=ALU.mult),
                reads=["pb2", "ln_bc"], writes=[f"hid{fo}"])
    p.dma("sync", b["rms_c"][0:128, 3:4], posidx_ap[0:128], reads=[pos_r], writes=["rms_c"], stream="posld", tag=f"pos{uid}a")
    p.dma("sync", b["rms_c"][0:128, 4:5], posidx_ap[128:256], reads=[pos_r], writes=["rms_c"], stream="posld", tag=f"pos{uid}b")
    p.op("vector", lambda e: e.tensor_scalar_add(b["rms_c"][0:128, 3:5], b["rms_c"][0:128, 3:5], -1.0),
         reads=["rms_c"], writes=["rms_c"])
    p.op("vector", lambda e: e.tensor_copy(b["posi32"][0:128, 0:2], b["rms_c"][0:128, 3:5]),
         reads=["rms_c"], writes=["posi32"])
    from concourse.bass import IndirectOffsetOnAxis
    for fo in range(8):
        sl = fo % 2
        wreg = f"w1s{sl}"
        p.dma("gpsimd", b[wreg][:, :], wout_ap[fo], writes=[wreg], tag=f"go{uid}{fo}")
        def mm_o(e, wreg=wreg):
            r = None
            for fi in range(8):
                r = e.matmul(b["pb0"][:, 0:NQ], b[wreg][:, fi * 128:(fi + 1) * 128],
                             xq_fm[fi][:, 512:512 + NQ], start=fi == 0, stop=fi == 7)
            return r
        p.mm(mm_o, reads=[wreg] + xr_q, writes=["pb0"])
        p.op("vector", lambda e: e.tensor_copy(b["ln_tmp"][:, 0:NQ], b["pb0"][:, 0:NQ]),
             reads=["pb0"], writes=["ln_tmp"])
        for qt in range(2):
            p.mm(lambda e, qt=qt: e.transpose(b["pb5"][:, 0:128], b["ln_tmp"][:, qt * 128:(qt + 1) * 128],
                                              b["ident_f32"][:, :]),
                 reads=["ln_tmp", "ident_f32"], writes=["pb5"])
            p.op("vector", lambda e, qt=qt, fo=fo: e.tensor_copy(
                b["lkv"][:, qt * 1024 + fo * 128:qt * 1024 + fo * 128 + 128], b["pb5"][:, 0:128]),
                reads=["pb5"], writes=["lkv"])
    for qt in range(2):
        p.op("gpsimd", lambda e, qt=qt: e.indirect_dma_start(
            acc_ap, IndirectOffsetOnAxis(b["posi32"][0:128, qt:qt + 1], 0),
            b["lkv"][:, qt * 1024:(qt + 1) * 1024], None,
            compute_op=ALU.add, oob_is_err=False),
            reads=["lkv", "posi32", acc_r], writes=[acc_r], dma=True, stream=acc_r,
            tag=f"accsc{uid}{qt}")


def emit_ln_fm_bf(ctx, x, x_r, xsl, dst, dst_r, dsl, T, gb=False, eps=1e-5):
    p = ctx.p
    b = ctx.b
    ones_col = b["ones_col_bf"]
    ones_row = b["ones_row_bf"]
    rows, bc, tmp = b["ln_rows"], b["ln_bc"], b["ln_tmp"]
    pr, pr2, pbc = b["pb4"], b["pb5"], b["pb6"]
    sq = b["ln_sq"]
    for nt in range(cdiv(T, 512)):
        t0 = nt * 512
        w = min(512, T - t0)
        for c in range(8):
            first, last = c == 0, c == 7
            p.op("scalar", lambda e, c=c, t0=t0, w=w: e.activation(
                sq[:, :w], x[c][:, xsl + t0:xsl + t0 + w], AF.Square),
                reads=[x_r[c]], writes=["ln_sq"])
            p.mm(lambda e, c=c, t0=t0, w=w, first=first, last=last: e.matmul(
                pr[0:1, :w], ones_col[:, 0:1], x[c][:, xsl + t0:xsl + t0 + w],
                start=first, stop=last),
                reads=[x_r[c]], writes=["pb4"])
            p.mm(lambda e, w=w, first=first, last=last: e.matmul(
                pr2[0:1, :w], ones_col[:, 0:1], sq[:, :w], start=first, stop=last),
                reads=["ln_sq"], writes=["pb5"])
        p.op("scalar", lambda e, w=w: e.activation(rows[:, 0:w], pr[0:1, :w], AF.Copy, scale=1.0 / 1024),
             reads=["pb4"], writes=["ln_rows"])
        p.op("scalar", lambda e, w=w: e.activation(rows[:, 1536:1536 + w], pr2[0:1, :w], AF.Copy, scale=1.0 / 1024),
             reads=["pb5"], writes=["ln_rows"])
        p.op("vector", lambda e, w=w: e.tensor_tensor(rows[:, 512:512 + w], rows[:, 0:w], rows[:, 0:w], op=ALU.mult),
             reads=["ln_rows"], writes=["ln_rows"])
        p.op("vector", lambda e, w=w: e.tensor_tensor(rows[:, 1536:1536 + w], rows[:, 1536:1536 + w], rows[:, 512:512 + w], op=ALU.subtract),
             reads=["ln_rows"], writes=["ln_rows"])
        p.op("scalar", lambda e, w=w: e.activation(rows[:, 512:512 + w], rows[:, 1536:1536 + w], AF.Sqrt, bias=b["eps_col"][0:1, 0:1]),
             reads=["ln_rows", "eps_col"], writes=["ln_rows"])
        p.op("vector", lambda e, w=w: e.reciprocal(rows[:, 512:512 + w], rows[:, 512:512 + w]),
             reads=["ln_rows"], writes=["ln_rows"])
        p.op("vector", lambda e, w=w: e.tensor_tensor(rows[:, 1024:1024 + w], rows[:, 0:w], rows[:, 512:512 + w], op=ALU.mult),
             reads=["ln_rows"], writes=["ln_rows"])
        p.op("vector", lambda e, w=w: e.tensor_scalar_mul(rows[:, 1024:1024 + w], rows[:, 1024:1024 + w], -1.0),
             reads=["ln_rows"], writes=["ln_rows"])
        p.mm(lambda e, w=w: e.matmul(pbc[:, :w], ones_row[0:1, :], rows[0:1, 512:512 + w], start=True, stop=True),
             reads=["ln_rows"], writes=["pb6"])
        p.op("vector", lambda e, w=w: e.tensor_copy(bc[:, 0:w], pbc[:, :w]),
             reads=["pb6"], writes=["ln_bc"])
        p.mm(lambda e, w=w: e.matmul(pbc[:, :w], ones_row[0:1, :], rows[0:1, 1024:1024 + w], start=True, stop=True),
             reads=["ln_rows"], writes=["pb6"])
        p.op("vector", lambda e, w=w: e.tensor_copy(bc[:, 512:512 + w], pbc[:, :w]),
             reads=["pb6"], writes=["ln_bc"])
        for c in range(8):
            p.op("vector", lambda e, c=c, t0=t0, w=w: e.tensor_tensor(
                tmp[:, :w], x[c][:, xsl + t0:xsl + t0 + w], bc[:, 0:w], op=ALU.mult),
                reads=[x_r[c], "ln_bc"], writes=["ln_tmp"])
            if gb:
                p.op("vector", lambda e, c=c, w=w: e.tensor_tensor(
                    tmp[:, :w], tmp[:, :w], bc[:, 512:512 + w], op=ALU.add),
                    reads=["ln_tmp", "ln_bc"], writes=["ln_tmp"])
                p.op("vector", lambda e, c=c, t0=t0, w=w: e.tensor_scalar(
                    dst[c][:, dsl + t0:dsl + t0 + w], tmp[:, :w], b["lng_c"][:, c:c + 1],
                    b["lnb_c"][:, c:c + 1], op0=ALU.mult, op1=ALU.add),
                    reads=["ln_tmp", "lng_c", "lnb_c"], writes=[dst_r[c]])
            else:
                p.op("vector", lambda e, c=c, t0=t0, w=w: e.tensor_tensor(
                    dst[c][:, dsl + t0:dsl + t0 + w], tmp[:, :w], bc[:, 512:512 + w], op=ALU.add),
                    reads=["ln_tmp", "ln_bc"], writes=[dst_r[c]])


def emit_local_mha_combine(ctx, xln, xln_r, resid, resid_r, own_col0, prev_col0,
                           prev_xg_ap, w_abs, ropeoff, wqkv_ap, wlo_ap,
                           acc_ap, acc_r, qsel_slot, uid):
    p = ctx.p
    b = ctx.b
    xn = [b[f"hid{16 + c}"] for c in range(6)] + [b["va0"], b["va1"]]
    xn_r = [f"hid{16 + c}" for c in range(6)] + ["va0", "va1"]
    latt = [b[f"hid{c}"] for c in range(8)]
    latt_r = [f"hid{c}" for c in range(8)]
    accf = [b[f"hid{8 + c}"] for c in range(8)]
    accf_r = [f"hid{8 + c}" for c in range(8)]
    emit_ln_fm_bf(ctx, xln, [xln_r + str(c) for c in range(8)], own_col0, xn, xn_r, 512, 512, gb=True)
    if prev_col0 is not None:
        emit_ln_fm_bf(ctx, xln, [xln_r + str(c) for c in range(8)], prev_col0, xn, xn_r, 0, 512, gb=True)
    elif prev_xg_ap is None:
        for c in range(8):
            p.op("vector", lambda e, c=c: e.memset(xn[c][:, 0:512], 0.0), writes=[xn_r[c]])
    else:
        for tile in range(4):
            p.dma("sync", b["xtile"][0:128, :], prev_xg_ap[tile * 128:(tile + 1) * 128, :],
                  writes=["xtile"], stream="xtile", tag=f"lp{uid}{tile}")
            for fc in range(8):
                p.mm(lambda e, fc=fc: e.transpose(b["pbt"][:, 0:128], b["xtile"][:, fc * 128:(fc + 1) * 128],
                                                  b["ident_bf"][:, :]),
                     reads=["xtile", "ident_bf"], writes=["pbt"])
                p.op("vector", lambda e, fc=fc, tile=tile: e.tensor_copy(
                    xn[fc][:, tile * 128:(tile + 1) * 128], b["pbt"][:, 0:128]),
                    reads=["pbt"], writes=[xn_r[fc]])
        emit_ln_fm_bf(ctx, xn, xn_r, 0, xn, xn_r, 0, 512, gb=True)

    for fo in range(8):
        for pi, off in ((0, 0), (1, 8), (2, 16)):
            sl = pi % 2
            wreg = f"w1s{sl}"
            p.dma("gpsimd", b[wreg][:, :], wqkv_ap[off + fo], writes=[wreg], tag=f"lm{uid}{fo}{pi}")
            ranges = [(512, 512)] if pi == 0 else [(0, 512), (512, 512)]
            for c0, cw in ranges:
                pa = f"pb{(c0 // 512) % 2}"
                def mm_p(e, wreg=wreg, c0=c0, cw=cw, pa=pa):
                    r = None
                    for fi in range(8):
                        r = e.matmul(b[pa][:, 0:cw], b[wreg][:, fi * 128:(fi + 1) * 128],
                                     xn[fi][:, c0:c0 + cw], start=fi == 0, stop=fi == 7)
                    return r
                p.mm(mm_p, reads=[wreg] + xn_r, writes=[pa])
                dstb, dstr = (b["lq"], "lq") if pi == 0 else (b["lkv"], "lkv")
                dc0 = 0 if pi == 0 else (c0 if pi == 1 else 1024 + c0)
                p.op("scalar", lambda e, pa=pa, cw=cw, dstb=dstb, dc0=dc0: e.activation(
                    dstb[:, dc0:dc0 + cw], b[pa][:, 0:cw], AF.Copy),
                    reads=[pa], writes=[dstr])
        p.mm(lambda e: e.matmul(b["pb0"][:, 0:512], b["rmat"][:, :], b["lq"][:, 0:512],
                                start=True, stop=True),
             reads=["rmat", "lq"], writes=["pb0"])
        p.op("vector", lambda e: e.tensor_tensor(b["ln_bc"][:, 0:512], b["pb0"][:, 0:512],
                                                 b["ropeS"][:, ropeoff + 512:ropeoff + 1024], op=ALU.mult),
             reads=["pb0", "ropeS"], writes=["ln_bc"])
        p.op("vector", lambda e: e.tensor_tensor(b["ln_tmp"][:, 0:512], b["lq"][:, 0:512],
                                                 b["ropeC"][:, ropeoff + 512:ropeoff + 1024], op=ALU.mult),
             reads=["lq", "ropeC"], writes=["ln_tmp"])
        p.op("vector", lambda e: e.tensor_tensor(b["lq"][:, 0:512], b["ln_bc"][:, 0:512],
                                                 b["ln_tmp"][:, 0:512], op=ALU.add),
             reads=["ln_bc", "ln_tmp"], writes=["lq"])
        for half in range(2):
            p.mm(lambda e, half=half: e.matmul(b["pb0"][:, 0:512], b["rmat"][:, :],
                                               b["lkv"][:, half * 512:(half + 1) * 512],
                                               start=True, stop=True),
                 reads=["rmat", "lkv"], writes=["pb0"])
            p.op("vector", lambda e, half=half: e.tensor_tensor(
                b["ln_bc"][:, 0:512], b["pb0"][:, 0:512],
                b["ropeS"][:, ropeoff + half * 512:ropeoff + half * 512 + 512], op=ALU.mult),
                reads=["pb0", "ropeS"], writes=["ln_bc"])
            p.op("vector", lambda e, half=half: e.tensor_tensor(
                b["ln_tmp"][:, 0:512], b["lkv"][:, half * 512:(half + 1) * 512],
                b["ropeC"][:, ropeoff + half * 512:ropeoff + half * 512 + 512], op=ALU.mult),
                reads=["lkv", "ropeC"], writes=["ln_tmp"])
            p.op("vector", lambda e, half=half: e.tensor_tensor(
                b["lkv"][:, half * 512:(half + 1) * 512], b["ln_bc"][:, 0:512],
                b["ln_tmp"][:, 0:512], op=ALU.add),
                reads=["ln_bc", "ln_tmp"], writes=["lkv"])
        for kt in range(8):
            p.mm(lambda e, kt=kt: e.transpose(b["pbt"][:, 0:128], b["lkv"][:, 1024 + kt * 128:1024 + (kt + 1) * 128],
                                              b["ident_bf"][:, :]),
                 reads=["lkv", "ident_bf"], writes=["pbt"])
            p.op("vector", lambda e, kt=kt: e.tensor_copy(b["vtm_r"][:, kt * 128:(kt + 1) * 128],
                                                          b["pbt"][:, 0:128]),
                 reads=["pbt"], writes=["vtm_r"])
        kt0 = 4 if w_abs == 0 else 0
        for hh in range(2):
            hs = hh * 64
            for kt in range(kt0, 8):
                p.mm(lambda e, kt=kt, hs=hs: e.matmul(
                    b["pb0"][:, 0:512], b["lkv"][hs:hs + 64, kt * 128:(kt + 1) * 128],
                    b["lq"][hs:hs + 64, 0:512], start=True, stop=True),
                    reads=["lkv", "lq"], writes=["pb0"])
                p.op("scalar", lambda e: e.activation(b["cwrk"][:, 0:512], b["pb0"][:, 0:512],
                                                      AF.Exp, scale=0.125),
                     reads=["pb0"], writes=["cwrk"])
                p.op("vector", lambda e, kt=kt: e.tensor_tensor(
                    b["cwrk"][:, 0:512], b["cwrk"][:, 0:512], b["kmask"][:, kt * 512:(kt + 1) * 512],
                    op=ALU.mult),
                    reads=["cwrk", "kmask"], writes=["cwrk"])
                p.mm(lambda e, kt=kt, first=kt == kt0: e.matmul(
                    b["pb4"][0:1, 0:512], b["ones_col_bf"][:, 0:1], b["cwrk"][:, 0:512],
                    start=first, stop=kt == 7),
                    reads=["cwrk", "ones_col_bf"], writes=["pb4"])
                p.mm(lambda e, kt=kt, hs=hs, first=kt == kt0: e.matmul(
                    b["pb2"][0:64, 0:512], b["vtm_r"][:, kt * 128 + hs:kt * 128 + hs + 64],
                    b["cwrk"][:, 0:512], start=first, stop=kt == 7),
                    reads=["vtm_r", "cwrk"], writes=["pb2"])
            p.op("vector", lambda e: e.tensor_copy(b["ln_rows"][0:1, 0:512], b["pb4"][0:1, 0:512]),
                 reads=["pb4"], writes=["ln_rows"])
            p.op("vector", lambda e: e.reciprocal(b["ln_rows"][0:1, 0:512], b["ln_rows"][0:1, 0:512]),
                 reads=["ln_rows"], writes=["ln_rows"])
            p.mm(lambda e: e.matmul(b["pb5"][0:64, 0:512], b["onesr128"][0:1, 0:64],
                                    b["ln_rows"][0:1, 0:512], start=True, stop=True),
                 reads=["ln_rows", "onesr128"], writes=["pb5"])
            p.op("vector", lambda e: e.tensor_copy(b["ln_bc"][0:64, 0:512], b["pb5"][0:64, 0:512]),
                 reads=["pb5"], writes=["ln_bc"])
            p.op("vector", lambda e, hs=hs, fo=fo: e.tensor_tensor(
                latt[fo][hs:hs + 64, 0:512], b["pb2"][0:64, 0:512], b["ln_bc"][0:64, 0:512],
                op=ALU.mult),
                reads=["pb2", "ln_bc"], writes=[latt_r[fo]])

    for tile in range(4):
        p.dma("sync", b["lkv"][0:128, 0:1024], acc_ap[tile * 128:(tile + 1) * 128, :],
              reads=[acc_r], writes=["lkv"], stream="accld", tag=f"acl{uid}{tile}")
        p.op("vector", lambda e: e.tensor_copy(b["cwrk"][:, 0:1024], b["lkv"][:, 0:1024]),
             reads=["lkv"], writes=["cwrk"])
        for fc in range(8):
            p.mm(lambda e, fc=fc: e.transpose(b["pbt"][:, 0:128], b["cwrk"][:, fc * 128:(fc + 1) * 128],
                                              b["ident_bf"][:, :]),
                 reads=["cwrk", "ident_bf"], writes=["pbt"])
            p.op("vector", lambda e, fc=fc, tile=tile: e.tensor_copy(
                accf[fc][:, tile * 128:(tile + 1) * 128], b["pbt"][:, 0:128]),
                reads=["pbt"], writes=[accf_r[fc]])
    p.op("vector", lambda e: e.tensor_scalar_add(
        b["skq"][0:4, 0:128], b["qsel_save"][0:4, qsel_slot * 128:qsel_slot * 128 + 128], 1.0),
        reads=["qsel_save"], writes=["skq"])
    p.op("vector", lambda e: e.reciprocal(b["skq"][0:4, 0:128], b["skq"][0:4, 0:128]),
         reads=["skq"], writes=["skq"])
    p.op("vector", lambda e: e.tensor_copy(b["skq_bf"][0:4, 0:128], b["skq"][0:4, 0:128]),
         reads=["skq"], writes=["skq_bf"])
    for cb in range(4):
        p.mm(lambda e, cb=cb: e.matmul(b["pb5"][:, 0:128], b["rsel4"][0:4, cb * 128:(cb + 1) * 128],
                                       b["skq_bf"][0:4, 0:128], start=True, stop=True),
             reads=["rsel4", "skq_bf"], writes=["pb5"])
        p.op("vector", lambda e, cb=cb: e.tensor_copy(b["ln_bc"][:, cb * 128:(cb + 1) * 128],
                                                      b["pb5"][:, 0:128]),
             reads=["pb5"], writes=["ln_bc"])
    for fo in range(8):
        sl = fo % 2
        wreg = f"w1s{sl}"
        p.dma("gpsimd", b[wreg][:, :], wlo_ap[fo], writes=[wreg], tag=f"lo{uid}{fo}")
        def mm_lo(e, wreg=wreg):
            r = None
            for fi in range(8):
                r = e.matmul(b["pb0"][:, 0:512], b[wreg][:, fi * 128:(fi + 1) * 128],
                             latt[fi][:, 0:512], start=fi == 0, stop=fi == 7)
            return r
        p.mm(mm_lo, reads=[wreg] + latt_r, writes=["pb0"])
        p.op("vector", lambda e, fo=fo: e.scalar_tensor_tensor(
            b["ln_tmp"][:, 0:512], b["pb0"][:, 0:512], 1.0, accf[fo][:, 0:512],
            op0=ALU.mult, op1=ALU.add),
            reads=["pb0", accf_r[fo]], writes=["ln_tmp"])
        p.op("vector", lambda e: e.tensor_tensor(b["ln_tmp"][:, 0:512], b["ln_tmp"][:, 0:512],
                                                 b["ln_bc"][:, 0:512], op=ALU.mult),
             reads=["ln_tmp", "ln_bc"], writes=["ln_tmp"])
        p.op("vector", lambda e, fo=fo: e.tensor_tensor(
            resid[fo][:, own_col0:own_col0 + 512], resid[fo][:, own_col0:own_col0 + 512],
            b["ln_tmp"][:, 0:512], op=ALU.add),
            reads=["ln_tmp", resid_r + str(fo)], writes=[resid_r + str(fo)])


def emit_l1ffn(ctx, resid, resid_r, xln, xln_r, T, n_seq, core_seq0,
               xg_ap, s_all_ap, dram, wqkv_ap, wlo_ap, wq_g, wkv_g, wout_g):
    p = ctx.p
    b = ctx.b
    HB = T // 2
    NPK = (n_seq - 512) // 128
    import math
    logk_kv = math.log(min(1024 * 9 / 8.0, float(n_seq - 512)))
    logk_q = math.log(min(256 * 9 / 8.0, 512.0))

    emit_ln_fm(ctx, resid, resid_r, xln, xln_r, T)

    n_win_local = HB // 512
    for bi in range(2):
        for wl in range(n_win_local):
            w_abs = core_seq0 // 512 + wl
            own_col0 = bi * HB + wl * 512
            qslot = bi * n_win_local + wl
            uid = f"{bi}{wl}"
            p.op("vector", lambda e: e.memset(b["cwrk"][:, 0:1024], 0.0), writes=["cwrk", "zrow_alias"])
            for r in range(0, 520, 128):
                nr = min(128, 520 - r)
                p.dma("gpsimd", dram["acc"][r:r + nr, :], b["cwrk"][0:nr, 0:1024],
                      reads=["zrow_alias", "cwrk"], writes=["acc"], stream="acc", tag=f"z{uid}{r}")
            if w_abs >= 1:
                for g in range(2):
                    emit_router_pack(ctx,
                                     s_all_ap[g, bi * n_seq + w_abs * 512:bi * n_seq + (w_abs + 1) * 512],
                                     None, 4, 128, logk_q, None, None, "kap60", "kap60", uid + f"q{g}")
                    emit_topk_sel(ctx, 4, 128, 256, "kap60", "kap60", "selq", "selq", uid=uid + f"q{g}")
                    if g == 0:
                        p.op("vector", lambda e, qslot=qslot: e.tensor_copy(
                            b["qsel_save"][0:4, qslot * 128:qslot * 128 + 128],
                            b["selq"][0:4, 0:128]),
                            reads=["selq"], writes=["qsel_save"])
                    else:
                        p.op("vector", lambda e, qslot=qslot: e.tensor_tensor(
                            b["qsel_save"][0:4, qslot * 128:qslot * 128 + 128],
                            b["qsel_save"][0:4, qslot * 128:qslot * 128 + 128],
                            b["selq"][0:4, 0:128], op=ALU.add),
                            reads=["selq", "qsel_save"], writes=["qsel_save"])
                    emit_rank_scatter(ctx, 4, "selq",
                                      lambda c, bi=bi, w_abs=w_abs: xg_ap[
                                          bi * n_seq + w_abs * 512 + c * 128:bi * n_seq + w_abs * 512 + (c + 1) * 128, :],
                                      dram[f"qbuf{g}"], f"qbuf{g}", 4, uid + f"q{g}",
                                      pos_dst_ap=dram[f"postab{g}"], pos_r=f"postab{g}", n_slots=256)
                    if w_abs <= 2:
                        for blk in range(8):
                            p.dma("sync", b["xtile"][:, :],
                                  xg_ap[bi * n_seq + blk * 128:bi * n_seq + (blk + 1) * 128, :],
                                  reads=["xg"], writes=["xtile"], stream="xtile",
                                  tag=f"kvl{uid}{g}{blk}")
                            p.dma("gpsimd", dram[f"kvbuf{g}"][blk * 128:(blk + 1) * 128, :],
                                  b["xtile"][:, :],
                                  reads=["xtile"], writes=[f"kvbuf{g}"], stream=f"kvbuf{g}",
                                  tag=f"kvst{uid}{g}{blk}")
                    else:
                        emit_router_pack(ctx,
                                         s_all_ap[2 + g, bi * n_seq:bi * n_seq + NPK * 128],
                                         dram[f"kvmask{wl}"], NPK, 128, logk_kv, None, None,
                                         "kap60", "kap60", uid + f"k{g}")
                        emit_topk_sel(ctx, NPK, 128, 1024, "kap60", "kap60", "selkv", "selkv",
                                      uid=uid + f"k{g}")
                        emit_rank_scatter(ctx, NPK, "selkv",
                                          lambda c, bi=bi: xg_ap[bi * n_seq + c * 128:bi * n_seq + (c + 1) * 128, :],
                                          dram[f"kvbuf{g}"], f"kvbuf{g}", min(w_abs * 4, NPK),
                                          uid + f"k{g}")
                for g in range(2):
                    n_live = min(w_abs * 4, 8)
                    emit_grouped_attn(ctx, dram[f"kvbuf{g}"], dram[f"qbuf{g}"], n_live, g,
                                      wq_g[g], wkv_g[g], wout_g[g], dram["acc"], "acc",
                                      dram[f"postab{g}"], uid + f"g{g}", pos_r=f"postab{g}",
                                      kv_r=f"kvbuf{g}", q_r=f"qbuf{g}")
            else:
                p.op("vector", lambda e, qslot=qslot: e.memset(
                    b["qsel_save"][0:4, qslot * 128:(qslot + 1) * 128], 0.0),
                    writes=["qsel_save"])
            if wl > 0:
                prev_col0, prev_xg = own_col0 - 512, None
            elif w_abs == 0:
                prev_col0, prev_xg = None, None
            else:
                prev_col0 = None
                prev_xg = xg_ap[bi * n_seq + (w_abs - 1) * 512:bi * n_seq + w_abs * 512, :]
            ropeoff = wl * 512
            emit_local_mha_combine(ctx, xln, xln_r, resid, resid_r, own_col0, prev_col0,
                                   prev_xg, w_abs, ropeoff, wqkv_ap, wlo_ap,
                                   dram["acc"], "acc", qslot, uid)


def l1ffn_consts(core_seq0, n_seq, n_win_local):
    import ml_dtypes
    bf = ml_dtypes.bfloat16
    out = {}
    NPK = (n_seq - 512) // 128
    io = np.zeros((60, 128), np.float32)
    for pp in range(60):
        io[pp] = pp * 128 + np.arange(128)
    out["c_iota128"] = io
    lt = np.zeros((60, 60), np.float32)
    for i in range(60):
        lt[:i, i] = 1.0
    out["c_lt60"] = lt
    for wl in range(n_win_local):
        w_abs = core_seq0 // 512 + wl
        m = np.full((NPK, 128), -1e30, np.float32)
        nvalid = min(w_abs * 512, NPK * 128)
        flat = np.arange(NPK * 128).reshape(NPK, 128)
        m[flat < nvalid] = 0.0
        out[f"c_kvmask{wl}"] = m
    qp = np.zeros((128, 4), np.float32)
    for c in range(4):
        qp[:, c] = c * 128 + np.arange(128) + 1
    out["c_qpos1"] = qp
    rs = np.zeros((4, 4, 128), np.float32)
    for c in range(4):
        rs[c, c, :] = 1.0
    out["c_rsel4"] = np.ascontiguousarray(rs.transpose(1, 0, 2).reshape(4, 512)).astype(bf)
    npos = 512 + n_win_local * 512
    pos = np.arange(core_seq0 - 512, core_seq0 + n_win_local * 512, dtype=np.float64)
    inv = 1.0 / 10000.0 ** (np.arange(0, 64, 2, dtype=np.float64) / 64)
    fr = pos[:, None] * inv[None, :]
    fr = np.concatenate([fr, fr], 1)
    cos = np.cos(fr).astype(np.float32)
    sin = np.sin(fr).astype(np.float32)
    cosf = np.concatenate([cos.T, cos.T], 0)
    sinf = np.concatenate([sin.T, sin.T], 0)
    out["c_ropeC"] = np.ascontiguousarray(cosf).astype(bf)
    out["c_ropeS"] = np.ascontiguousarray(sinf).astype(bf)
    R = np.zeros((64, 64), np.float32)
    for pp in range(32):
        R[pp + 32, pp] = -1.0
        R[pp, pp + 32] = 1.0
    R2 = np.zeros((128, 128), np.float32)
    R2[0:64, 0:64] = R
    R2[64:128, 64:128] = R
    out["c_rmat"] = R2.astype(bf)
    km = np.zeros((8, 128, 512), np.float32)
    for kt in range(8):
        for kk in range(128):
            kpos = kt * 128 + kk - 512
            for qq in range(512):
                if kpos <= qq and qq - kpos <= 512:
                    km[kt, kk, qq] = 1.0
    out["c_kmask"] = np.ascontiguousarray(km.transpose(1, 0, 2).reshape(128, 8 * 512)).astype(bf)
    return out



def _alloc_common(ctx, T):
    resid = [ctx.sb(f"res{c}", [128, T], F32) for c in range(8)]
    xln = [ctx.sb(f"xln{c}", [128, T], BF16) for c in range(8)]
    alloc_psum_pool(ctx)
    alloc_ln_scratch(ctx, T)
    return resid, xln


def _init_ident_f32(ctx):
    p = ctx.p
    p.op("gpsimd", lambda e: e.memset(ctx.b["ident_f32"][:], 1.0), writes=["ident_f32"])
    p.op("gpsimd", lambda e: e.affine_select(
        out=ctx.b["ident_f32"][:], in_=ctx.b["ident_f32"][:], pattern=[[1, 128]],
        compare_op=ALU.is_equal, fill=0.0, channel_multiplier=-1),
        reads=["ident_f32"], writes=["ident_f32"])


def _init_ident_bf(ctx):
    p = ctx.p
    p.op("gpsimd", lambda e: e.memset(ctx.b["ident_bf"][:], 1.0), writes=["ident_bf"])
    p.op("gpsimd", lambda e: e.affine_select(
        out=ctx.b["ident_bf"][:], in_=ctx.b["ident_bf"][:], pattern=[[1, 128]],
        compare_op=ALU.is_equal, fill=0.0, channel_multiplier=-1),
        reads=["ident_bf"], writes=["ident_bf"])


def _load_x(ctx, x_ap, resid, T):
    for c in range(8):
        ctx.p.dma("sync", resid[c][:, :], x_ap[c * 128:(c + 1) * 128, :], writes=[f"res{c}"])


def _store_x(ctx, y_ap, resid, T):
    for c in range(8):
        ctx.p.dma("sync", y_ap[c * 128:(c + 1) * 128, :], resid[c][:, :],
                  reads=[f"res{c}"], writes=[f"y{c}"])


def _w8(nc, name, n):
    return nc.dram_tensor(name, [n, 128, 1024], F32, kind="ExternalInput").ap()


def _gla_io(nc):
    return (_w8(nc, "g_wq", 8), _w8(nc, "g_wk", 8), _w8(nc, "g_wv", 8), _w8(nc, "g_wo", 8),
            nc.dram_tensor("g_wf1", [128, 512], F32, kind="ExternalInput").ap(),
            nc.dram_tensor("g_wf2", [64, 1024], F32, kind="ExternalInput").ap(),
            nc.dram_tensor("g_wg1", [128, 512], F32, kind="ExternalInput").ap(),
            nc.dram_tensor("g_wg2", [64, 1024], F32, kind="ExternalInput").ap(),
            nc.dram_tensor("g_nw", [128, 8], F32, kind="ExternalInput").ap())


def _gla_alloc(ctx):
    for j in range(2):
        ctx.sb(f"w1s{j}", [128, 8 * 128], BF16)
    for hc in range(16):
        ctx.sb(f"hid{hc}", [128, 1024], BF16)
    ctx.sb("ec_sb", [128, 8 * 16], BF16)
    ctx.sb("fc_sb", [16, 8 * 128], BF16)
    alloc_gla_scratch(ctx)


def build_A1(T=2048):
    nc = bass.Bass("TRN2", target_bir_lowering=False)
    x_d = nc.dram_tensor("x", [1024, T], F32, kind="ExternalInput").ap()
    wq_d, wk_d, wv_d, wo_d, wf1_d, wf2_d, wg1_d, wg2_d, nw_d = _gla_io(nc)
    ec_d = nc.dram_tensor("c_ec", [128, 128], BF16, kind="ExternalInput").ap()
    fc_d = nc.dram_tensor("c_fc", [16, 1024], BF16, kind="ExternalInput").ap()
    gsum_d = nc.dram_tensor("gsum", [2, 128, 520], F32, kind="ExternalOutput").ap()
    with ExitStack() as es:
        ctx = Ctx(nc, es)
        p = ctx.p
        resid, xln = _alloc_common(ctx, T)
        _gla_alloc(ctx)
        p.dma("sync", ctx.b["ec_sb"][:, :], ec_d, writes=["ec_sb"])
        p.dma("sync", ctx.b["fc_sb"][:, :], fc_d, writes=["fc_sb"])
        _load_x(ctx, x_d, resid, T)
        emit_ln_fm(ctx, resid, "res", xln, "xln", T)
        emit_gla_pass(ctx, xln, "xln", resid, "res", False,
                      wq_d, wk_d, wv_d, wo_d, wf1_d, wf2_d, wg1_d, wg2_d, nw_d,
                      gsum_out_ap=gsum_d, T=T)
        p.finalize(final_regions=["gsum_out"])
    return nc


def build_A2(T=2048):
    nc = bass.Bass("TRN2", target_bir_lowering=False)
    x_d = nc.dram_tensor("x", [1024, T], F32, kind="ExternalInput").ap()
    summ_d = nc.dram_tensor("summ_all", [7, 2, 128, 520], F32, kind="ExternalInput").ap()
    wq_d, wk_d, wv_d, wo_d, wf1_d, wf2_d, wg1_d, wg2_d, nw_d = _gla_io(nc)
    ec_d = nc.dram_tensor("c_ec", [128, 128], BF16, kind="ExternalInput").ap()
    fc_d = nc.dram_tensor("c_fc", [16, 1024], BF16, kind="ExternalInput").ap()
    y_d = nc.dram_tensor("y", [1024, T], F32, kind="ExternalOutput").ap()
    with ExitStack() as es:
        ctx = Ctx(nc, es)
        p = ctx.p
        resid, xln = _alloc_common(ctx, T)
        _gla_alloc(ctx)
        p.dma("sync", ctx.b["ec_sb"][:, :], ec_d, writes=["ec_sb"])
        p.dma("sync", ctx.b["fc_sb"][:, :], fc_d, writes=["fc_sb"])
        _load_x(ctx, x_d, resid, T)
        emit_ln_fm(ctx, resid, "res", xln, "xln", T)
        emit_gla_fold(ctx, [[summ_d[j, bh] for bh in range(2)] for j in range(7)], 7)
        emit_gla_pass(ctx, xln, "xln", resid, "res", True,
                      wq_d, wk_d, wv_d, wo_d, wf1_d, wf2_d, wg1_d, wg2_d, nw_d, T=T)
        _store_x(ctx, y_d, resid, T)
        p.finalize(final_regions=[f"y{c}" for c in range(8)])
    return nc


def build_B1(T=2048):
    nc = bass.Bass("TRN2", target_bir_lowering=False)
    x_d = nc.dram_tensor("x", [1024, T], F32, kind="ExternalInput").ap()
    w1_d = nc.dram_tensor("m_w1", [8, 22, 128, 8 * 128], F32, kind="ExternalInput").ap()
    w3_d = nc.dram_tensor("m_w3", [8, 22, 128, 8 * 128], F32, kind="ExternalInput").ap()
    w2_d = nc.dram_tensor("m_w2", [8, 8, 128, 22 * 128], F32, kind="ExternalInput").ap()
    sw1_d = nc.dram_tensor("m_sw1", [12, 128, 8 * 128], F32, kind="ExternalInput").ap()
    sw3_d = nc.dram_tensor("m_sw3", [12, 128, 8 * 128], F32, kind="ExternalInput").ap()
    sw2_d = nc.dram_tensor("m_sw2", [8, 128, 12 * 128], F32, kind="ExternalInput").ap()
    gate_d = nc.dram_tensor("m_gate", [128, 64], F32, kind="ExternalInput").ap()
    sel_d = nc.dram_tensor("m_sel8", [8, 8 * 128], BF16, kind="ExternalInput").ap()
    y_d = nc.dram_tensor("y", [1024, T], F32, kind="ExternalOutput").ap()
    with ExitStack() as es:
        ctx = Ctx(nc, es)
        p = ctx.p
        resid, xln = _alloc_common(ctx, T)
        alloc_moe_scratch(ctx, 1024)
        _init_ident_f32(ctx)
        _load_x(ctx, x_d, resid, T)
        p.dma("gpsimd", ctx.b["gate_sb"][:, :], gate_d, writes=["gate_sb"])
        p.dma("sync", ctx.b["sel8_sb"][:, :], sel_d, writes=["sel8_sb"])
        emit_ln_fm(ctx, resid, "res", xln, "xln", T)
        emit_l0ffn(ctx, resid, "res", xln, "xln", T, 1024, w1_d, w3_d, w2_d, sw1_d, sw3_d, sw2_d)
        _store_x(ctx, y_d, resid, T)
        p.finalize(final_regions=[f"y{c}" for c in range(8)])
    return nc


def build_B2(T=2048):
    nc = bass.Bass("TRN2", target_bir_lowering=False)
    x_d = nc.dram_tensor("x", [1024, T], F32, kind="ExternalInput").ap()
    aqw_d, akw_d = _w8(nc, "a_qw", 8), _w8(nc, "a_kw", 8)
    avw_d, aow_d = _w8(nc, "a_vw", 8), _w8(nc, "a_ow", 8)
    names = dict(a_grep_w=[8, 64], a_grep_b=[8, 1], a_grep_a=[16, 1], a_posb=[16, 4],
                 a_qb=[128, 8], a_kb=[128, 8], a_vb=[128, 8], a_ob=[128, 8],
                 a_shift24=[4, 4], a_asm=[4, 128], a_eac=[16, 32], a_ec=[128, 128],
                 a_fc=[16, 1024], a_halfsel=[8, 2], a_halfsel2=[8, 4], a_perm24=[2, 4])
    dts = dict(a_shift24=BF16, a_asm=BF16, a_ec=BF16, a_fc=BF16)
    daps = {n_: nc.dram_tensor(n_, s_, dts.get(n_, F32), kind="ExternalInput").ap()
            for n_, s_ in names.items()}
    rt_d = nc.dram_tensor("f_rt", [128, 32], F32, kind="ExternalInput").ap()
    y_d = nc.dram_tensor("y", [1024, T], F32, kind="ExternalOutput").ap()
    s_out_d = nc.dram_tensor("s_own", [4, T], F32, kind="ExternalOutput").ap()
    xtm_d = nc.dram_tensor("xtm_own", [T, 1024], BF16, kind="ExternalOutput").ap()
    with ExitStack() as es:
        ctx = Ctx(nc, es)
        p = ctx.p
        b = ctx.b
        resid, xln = _alloc_common(ctx, T)
        for j in range(2):
            ctx.sb(f"w1s{j}", [128, 8 * 128], BF16)
        for hc in range(22):
            ctx.sb(f"hid{hc}", [128, 1024], BF16)
        alloc_l1attn_scratch(ctx)
        ctx.sb("ident_bf", [128, 128], BF16)
        _init_ident_bf(ctx)
        ctx.sb("rt_f", [128, 32], F32)
        ctx.sb("rt_hif", [128, 32], F32)
        ctx.sb("rt_hi", [128, 32], BF16)
        ctx.sb("rt_lo", [128, 32], BF16)
        ctx.sb("s_st", [4, 512], F32)
        ctx.sb("xtile", [128, 1024], BF16)
        _load_x(ctx, x_d, resid, T)
        emit_l1attn_init(ctx, daps["a_grep_w"], daps["a_grep_b"], daps["a_grep_a"], daps["a_posb"],
                         daps["a_qb"], daps["a_kb"], daps["a_vb"], daps["a_ob"], daps["a_shift24"],
                         daps["a_asm"], daps["a_eac"], daps["a_ec"], daps["a_fc"], daps["a_halfsel"],
                         daps["a_halfsel2"], daps["a_perm24"])
        emit_ln_fm(ctx, resid, "res", xln, "xln", T)
        emit_l1attn(ctx, resid, "res", xln, "xln", T, aqw_d, akw_d, avw_d, aow_d)
        p.dma("sync", b["rt_f"][:, :], rt_d, writes=["rt_f"])
        p.op("vector", lambda e: e.tensor_copy(b["rt_hi"][:, :], b["rt_f"][:, :]),
             reads=["rt_f"], writes=["rt_hi"])
        p.op("vector", lambda e: e.tensor_copy(b["rt_hif"][:, :], b["rt_hi"][:, :]),
             reads=["rt_hi"], writes=["rt_hif"])
        p.op("vector", lambda e: e.tensor_tensor(b["rt_lo"][:, :], b["rt_f"][:, :],
                                                 b["rt_hif"][:, :], op=ALU.subtract),
             reads=["rt_f", "rt_hif"], writes=["rt_lo"])
        hid_list = [b[f"hid{c}"] for c in range(8)]
        for nt in range(T // 512):
            emit_ln_fm(ctx, resid, "res", xln, "xln", 512, lo=hid_list, lo_r="hid", xsl=nt * 512)
            def mm_s(e):
                r = None
                for fi in range(8):
                    e.matmul(b["pb4"][0:4, 0:512], b["rt_hi"][:, fi * 4:(fi + 1) * 4],
                             xln[fi][:, 0:512], start=fi == 0, stop=False)
                    e.matmul(b["pb4"][0:4, 0:512], b["rt_lo"][:, fi * 4:(fi + 1) * 4],
                             xln[fi][:, 0:512], start=False, stop=False)
                    r = e.matmul(b["pb4"][0:4, 0:512], b["rt_hi"][:, fi * 4:(fi + 1) * 4],
                                 b[f"hid{fi}"][:, 0:512], start=False, stop=fi == 7)
                return r
            p.mm(mm_s, reads=["rt_hi", "rt_lo"] + [f"xln{c}" for c in range(8)] +
                 [f"hid{c}" for c in range(8)], writes=["pb4"])
            p.op("vector", lambda e: e.tensor_copy(b["s_st"][0:4, 0:512], b["pb4"][0:4, 0:512]),
                 reads=["pb4"], writes=["s_st"])
            p.dma("sync", s_out_d[:, nt * 512:(nt + 1) * 512], b["s_st"][0:4, 0:512],
                  reads=["s_st"], writes=["s_out"], stream="s_out", tag=f"so{nt}")
            for tt in range(4):
                for fc in range(8):
                    p.mm(lambda e, tt=tt, fc=fc: e.transpose(
                        b["pbt"][:, 0:128], xln[fc][:, tt * 128:(tt + 1) * 128], b["ident_bf"][:, :]),
                        reads=[f"xln{fc}", "ident_bf"], writes=["pbt"])
                    p.op("vector", lambda e, fc=fc: e.tensor_copy(
                        b["xtile"][:, fc * 128:(fc + 1) * 128], b["pbt"][:, 0:128]),
                        reads=["pbt"], writes=["xtile"])
                p.dma("sync", xtm_d[nt * 512 + tt * 128:nt * 512 + (tt + 1) * 128, :],
                      b["xtile"][:, :], reads=["xtile"], writes=["xtm_out"],
                      stream="xtm_out", tag=f"xt{nt}_{tt}")
        _store_x(ctx, y_d, resid, T)
        p.finalize(final_regions=[f"y{c}" for c in range(8)] + ["s_out", "xtm_out"])
    return nc


def build_C(core_seq0, T=2048, n_seq=8192):
    nc = bass.Bass("TRN2", target_bir_lowering=False)
    x_d = nc.dram_tensor("x", [1024, T], F32, kind="ExternalInput").ap()
    xg_d = nc.dram_tensor("xg", [2 * n_seq, 1024], BF16, kind="ExternalInput").ap()
    s_d = nc.dram_tensor("s_all", [4, 2 * n_seq], F32, kind="ExternalInput").ap()
    wqkv_d = nc.dram_tensor("f_wqkv", [24, 128, 1024], F32, kind="ExternalInput").ap()
    wlo_d = nc.dram_tensor("f_wlo", [8, 128, 1024], F32, kind="ExternalInput").ap()
    wqg_d = nc.dram_tensor("f_wqg", [2, 8, 128, 1024], F32, kind="ExternalInput").ap()
    wkvg_d = nc.dram_tensor("f_wkvg", [2, 16, 128, 1024], F32, kind="ExternalInput").ap()
    woutg_d = nc.dram_tensor("f_woutg", [2, 8, 128, 1024], F32, kind="ExternalInput").ap()
    NPK = (n_seq - 512) // 128
    names = dict(c_iota128=[60, 128], c_lt60=[60, 60], c_qpos1=[128, 4], c_rsel4=[4, 512],
                 c_ropeC=[128, 1536], c_ropeS=[128, 1536], c_rmat=[128, 128],
                 c_kmask=[128, 4096], c_kvmask0=[NPK, 128], c_kvmask1=[NPK, 128],
                 c_cnb=[128, 2048], c_qnb=[128, 2048], c_lng=[128, 8], c_lnb=[128, 8],
                 c_nullk=[128, 32], c_nullv=[1, 2048])
    dts = dict(c_rsel4=BF16, c_ropeC=BF16, c_ropeS=BF16, c_rmat=BF16,
               c_kmask=BF16, c_cnb=BF16, c_qnb=BF16, c_nullk=BF16, c_nullv=BF16)
    daps = {n_: nc.dram_tensor(n_, s_, dts.get(n_, F32), kind="ExternalInput").ap()
            for n_, s_ in names.items()}
    y_d = nc.dram_tensor("y", [1024, T], F32, kind="ExternalOutput").ap()
    dram = dict(
        kvbuf0=nc.dram_tensor("kvbuf0", [1100, 1024], BF16).ap(),
        kvbuf1=nc.dram_tensor("kvbuf1", [1100, 1024], BF16).ap(),
        qbuf0=nc.dram_tensor("qbuf0", [300, 1024], BF16).ap(),
        qbuf1=nc.dram_tensor("qbuf1", [300, 1024], BF16).ap(),
        postab0=nc.dram_tensor("postab0", [300, 1], F32).ap(),
        postab1=nc.dram_tensor("postab1", [300, 1], F32).ap(),
        acc=nc.dram_tensor("accbuf", [520, 1024], BF16).ap(),
        kvmask0=daps["c_kvmask0"], kvmask1=daps["c_kvmask1"],
    )
    with ExitStack() as es:
        ctx = Ctx(nc, es)
        p = ctx.p
        resid, xln = _alloc_common(ctx, T)
        for j in range(2):
            ctx.sb(f"w1s{j}", [128, 8 * 128], BF16)
        for hc in range(22):
            ctx.sb(f"hid{hc}", [128, 1024], BF16)
        ctx.sb("ident_f32", [128, 128], F32)
        ctx.sb("va0", [128, 1024], BF16)
        ctx.sb("va1", [128, 1024], BF16)
        ctx.sb("ident_bf", [128, 128], BF16)
        _init_ident_bf(ctx)
        alloc_l1ffn_scratch(ctx)
        _init_ident_f32(ctx)
        p.op("vector", lambda e: e.memset(ctx.b["onesr128"][:, :], 1.0), writes=["onesr128"])
        p.op("vector", lambda e: e.memset(ctx.b["ones60"][:, :], 1.0), writes=["ones60"])
        for nm, buf in (("c_iota128", "iota128"), ("c_lt60", "lt60"), ("c_qpos1", "qpos1"),
                        ("c_rsel4", "rsel4"), ("c_ropeC", "ropeC"), ("c_ropeS", "ropeS"),
                        ("c_rmat", "rmat"), ("c_kmask", "kmask"),
                        ("c_lng", "lng_c"), ("c_lnb", "lnb_c"), ("c_nullk", "nullk_sb")):
            p.dma("sync", ctx.b[buf][:, :], daps[nm], writes=[buf])
        ctx.nullv_ap = daps["c_nullv"]
        ctx.cnb_ap = daps["c_cnb"]
        ctx.qnb_ap = daps["c_qnb"]
        _load_x(ctx, x_d, resid, T)
        emit_l1ffn(ctx, resid, "res", xln, "xln", T, n_seq, core_seq0,
                   xg_d, s_d, dram,
                   [wqkv_d[i] for i in range(24)], [wlo_d[i] for i in range(8)],
                   [[wqg_d[g, i] for i in range(8)] for g in range(2)],
                   [[wkvg_d[g, i] for i in range(16)] for g in range(2)],
                   [[woutg_d[g, i] for i in range(8)] for g in range(2)])
        _store_x(ctx, y_d, resid, T)
        p.finalize(final_regions=[f"y{c}" for c in range(8)])
    return nc



def _host_prep(ins):
    import ml_dtypes
    bf = ml_dtypes.bfloat16
    cons = l1attn_consts()
    posb = np.asarray(ins["L1_attn_rel_bias"])[np.array([[0, 17], [1, 0]])]
    posb = posb.transpose(2, 0, 1).reshape(16, 4)
    d = {}
    d["A"] = dict(
        g_wq=prelay_w(np.asarray(ins["L0_attn_Wq"]), 8, 8),
        g_wk=prelay_w(np.asarray(ins["L0_attn_Wk"]), 8, 8),
        g_wv=prelay_w(np.asarray(ins["L0_attn_Wv"]), 8, 8),
        g_wo=prelay_w(np.asarray(ins["L0_attn_Wo"]), 8, 8),
        g_wf1=prelay_w64(np.asarray(ins["L0_attn_Wf1"])),
        g_wf2=np.ascontiguousarray(np.asarray(ins["L0_attn_Wf2"])),
        g_wg1=prelay_w64(np.asarray(ins["L0_attn_Wg1"])),
        g_wg2=np.ascontiguousarray(np.asarray(ins["L0_attn_Wg2"])),
        g_nw=np.ascontiguousarray(np.asarray(ins["L0_attn_norm_weight"]).reshape(8, 128).T),
        c_ec=cons["l1_ec"], c_fc=cons["l1_fc"],
    )
    w1 = np.asarray(ins["L0_ffn_w1"]); w2 = np.asarray(ins["L0_ffn_w2"]); w3 = np.asarray(ins["L0_ffn_w3"])
    d["B1"] = dict(
        m_w1=np.stack([prelay_w(w1[e], 8, 22) for e in range(8)]),
        m_w3=np.stack([prelay_w(w3[e], 8, 22) for e in range(8)]),
        m_w2=np.stack([prelay_w(w2[e], 22, 8) for e in range(8)]),
        m_sw1=prelay_w(np.asarray(ins["L0_ffn_sw1"]), 8, 12),
        m_sw3=prelay_w(np.asarray(ins["L0_ffn_sw3"]), 8, 12),
        m_sw2=prelay_w(np.asarray(ins["L0_ffn_sw2"]), 12, 8),
        m_gate=prelay_gate(np.asarray(ins["L0_ffn_gate_w"])),
        m_sel8=sel8_np(),
    )
    rt_mat = np.stack([np.asarray(ins["L1_ffn_rt_q"])[0], np.asarray(ins["L1_ffn_rt_q"])[1],
                       np.asarray(ins["L1_ffn_rt_kv"])[0], np.asarray(ins["L1_ffn_rt_kv"])[1]], 1)
    rt_p = np.ascontiguousarray(rt_mat.reshape(8, 128, 4).transpose(1, 0, 2).reshape(128, 32))
    d["B2"] = dict(
        a_qw=prelay_w(np.ascontiguousarray(np.asarray(ins["L1_attn_q_w"]).T), 8, 8),
        a_kw=prelay_w(np.ascontiguousarray(np.asarray(ins["L1_attn_k_w"]).T), 8, 8),
        a_vw=prelay_w(np.ascontiguousarray(np.asarray(ins["L1_attn_v_w"]).T), 8, 8),
        a_ow=prelay_w(np.ascontiguousarray(np.asarray(ins["L1_attn_out_w"]).T), 8, 8),
        a_grep_w=np.asarray(ins["L1_attn_grep_w"]),
        a_grep_b=np.asarray(ins["L1_attn_grep_b"]).reshape(8, 1),
        a_grep_a=np.asarray(ins["L1_attn_grep_a"]).reshape(16, 1),
        a_posb=np.ascontiguousarray(posb),
        a_qb=np.ascontiguousarray(np.asarray(ins["L1_attn_q_b"]).reshape(8, 128).T),
        a_kb=np.ascontiguousarray(np.asarray(ins["L1_attn_k_b"]).reshape(8, 128).T),
        a_vb=np.ascontiguousarray(np.asarray(ins["L1_attn_v_b"]).reshape(8, 128).T),
        a_ob=np.ascontiguousarray(np.asarray(ins["L1_attn_out_b"]).reshape(8, 128).T),
        a_shift24=cons["l1_shift24"], a_asm=cons["l1_asm"], a_eac=cons["l1_eac"],
        a_ec=cons["l1_ec"], a_fc=cons["l1_fc"], a_halfsel=cons["l1_halfsel"],
        a_halfsel2=cons["l1_halfsel2"], a_perm24=cons["l1_perm24"],
        f_rt=rt_p,
    )
    wqkvT = np.ascontiguousarray(np.asarray(ins["L1_ffn_w_qkv"]).T)
    nullk = np.zeros((128, 32), np.float32)
    nullv = np.zeros((1, 2048), np.float32)
    nkv = np.asarray(ins["L1_ffn_null_kv"])
    for g in range(2):
        for h in range(16):
            r0 = (h % 2) * 64
            nullk[r0:r0 + 64, g * 16 + h] = nkv[0, g, h, 0]
            nullv[0, (g * 16 + h) * 64:(g * 16 + h) * 64 + 64] = nkv[1, g, h, 0]
    cnb = np.broadcast_to(np.concatenate([np.asarray(ins["L1_ffn_cnorm_g"])[0],
                                          np.asarray(ins["L1_ffn_cnorm_g"])[1]])[None, :],
                          (128, 2048)).astype(bf)
    qnb = np.broadcast_to(np.concatenate([np.asarray(ins["L1_ffn_qnorm_g"])[0],
                                          np.asarray(ins["L1_ffn_qnorm_g"])[1]])[None, :],
                          (128, 2048)).astype(bf)
    d["C"] = dict(
        f_wqkv=np.stack([prelay_w(wqkvT, 8, 24)[i] for i in range(24)]),
        f_wlo=prelay_w(np.ascontiguousarray(np.asarray(ins["L1_ffn_w_lo"]).T), 8, 8),
        f_wqg=np.stack([prelay_w(np.ascontiguousarray(np.asarray(ins["L1_ffn_w_q"])[g].T), 8, 8)
                        for g in range(2)]),
        f_wkvg=np.stack([prelay_w(np.ascontiguousarray(np.asarray(ins["L1_ffn_w_kv"])[g].T), 8, 16)
                         for g in range(2)]),
        f_woutg=np.stack([prelay_w(np.ascontiguousarray(np.asarray(ins["L1_ffn_w_out"])[g].T), 8, 8)
                          for g in range(2)]),
        c_cnb=np.ascontiguousarray(cnb), c_qnb=np.ascontiguousarray(qnb),
        c_lng=np.ascontiguousarray(np.asarray(ins["L1_ffn_ln_g"]).reshape(8, 128).T),
        c_lnb=np.ascontiguousarray(np.asarray(ins["L1_ffn_ln_b"]).reshape(8, 128).T),
        c_nullk=nullk.astype(bf), c_nullv=nullv.astype(bf),
    )
    return d


def kernel(**inputs):
    import ml_dtypes
    from concourse.bass_utils import run_bass_kernel_spmd
    bf = ml_dtypes.bfloat16
    NC = 8
    NSEQ = 8192
    SL = NSEQ // NC
    T = 2 * SL
    x = np.asarray(inputs["L0_attn_x"])
    w = _host_prep(inputs)

    def core_fm(xf):
        return [np.ascontiguousarray(
            np.concatenate([xf[0, c * SL:(c + 1) * SL], xf[1, c * SL:(c + 1) * SL]], 0).T)
            for c in range(NC)]

    x_cores = core_fm(x)

    ncA1 = build_A1(T)
    res = run_bass_kernel_spmd(ncA1, [dict(x=x_cores[c], **w["A"]) for c in range(NC)],
                               core_ids=list(range(NC)))
    gsums = [np.asarray(res.results[c]["gsum"]) for c in range(NC)]

    ident = np.zeros((2, 128, 520), np.float32)
    ident[:, :, 512:520] = 1.0
    ncA2 = build_A2(T)
    in_maps = []
    for c in range(NC):
        summ = np.stack([gsums[j] for j in range(c)] + [ident] * (7 - c)) if c > 0 \
            else np.stack([ident] * 7)
        in_maps.append(dict(x=x_cores[c], summ_all=summ, **w["A"]))
    res = run_bass_kernel_spmd(ncA2, in_maps, core_ids=list(range(NC)))
    x1 = [np.asarray(res.results[c]["y"]) for c in range(NC)]

    ncB1 = build_B1(T)
    res = run_bass_kernel_spmd(ncB1, [dict(x=x1[c], **w["B1"]) for c in range(NC)],
                               core_ids=list(range(NC)))
    x2 = [np.asarray(res.results[c]["y"]) for c in range(NC)]

    ncB2 = build_B2(T)
    res = run_bass_kernel_spmd(ncB2, [dict(x=x2[c], **w["B2"]) for c in range(NC)],
                               core_ids=list(range(NC)))
    x3 = [np.asarray(res.results[c]["y"]) for c in range(NC)]
    xg = np.empty((2 * NSEQ, 1024), bf)
    s_all = np.empty((4, 2 * NSEQ), np.float32)
    for c in range(NC):
        xtm = np.asarray(res.results[c]["xtm_own"])
        s_own = np.asarray(res.results[c]["s_own"])
        for b in range(2):
            xg[b * NSEQ + c * SL:b * NSEQ + (c + 1) * SL] = xtm[b * SL:(b + 1) * SL]
            s_all[:, b * NSEQ + c * SL:b * NSEQ + (c + 1) * SL] = s_own[:, b * SL:(b + 1) * SL]

    y_cores = []
    for c in range(NC):
        ncC = build_C(c * SL, T, NSEQ)
        res = run_bass_kernel_spmd(
            ncC, [dict(x=x3[c], xg=xg, s_all=s_all, **w["C"],
                       **l1ffn_consts(c * SL, NSEQ, SL // 512))], core_ids=[c])
        y_cores.append(np.asarray(res.results[0]["y"]))

    out = np.empty((2, NSEQ, 1024), np.float32)
    for c in range(NC):
        yt = y_cores[c].T
        out[0, c * SL:(c + 1) * SL] = yt[:SL]
        out[1, c * SL:(c + 1) * SL] = yt[SL:]
    return out
